# version 16
# baseline (speedup 1.0000x reference)
import math
from contextlib import ExitStack

import numpy as np
import ml_dtypes

import concourse.bass as bass
import concourse.mybir as mybir
from concourse.bass_core_rust import IndirectOffsetOnAxis
from concourse.bass_utils import run_bass_kernel_spmd

F32 = mybir.dt.float32
BF16 = mybir.dt.bfloat16
I32 = mybir.dt.int32
AF = mybir.ActivationFunctionType
OP = mybir.AluOpType
AX = mybir.AxisListType.X

NCORES = 8
T = 2048
SEQ = 8192
D = 1024
NH = 16
DH = 64
NQT = 4
QW = 512
NKB = 64
NE = 8
NRE = 7
CAP = 2048
DI = 4096
TBLR = 16384
DUMP_X = 14336
DUMP_G = 16384
BETA_KILL = -60000.0
NTOK = NCORES * T

ENGS = ("sy", "gp", "pe", "dv", "ac")


def _slopes16():
    start = 2.0 ** (-(2.0 ** -(math.log2(NH) - 3)))
    return [start ** i for i in range(NH)]


SLOPES = _slopes16()


def _host_tables(core):
    my_base = (core % 4) * T
    beta = np.zeros((NH, 128, 257), np.float32)
    p = np.arange(128, dtype=np.float64)
    for h in range(NH):
        s = SLOPES[h]
        for qt in range(NQT):
            i0 = my_base + QW * qt
            for bk in range(NKB):
                j0 = 128 * bk
                col = qt * NKB + bk
                if i0 <= j0 < i0 + QW:
                    beta[h, :, col] = BETA_KILL
                elif j0 + 127 < i0:
                    beta[h, :, col] = (-s * (i0 - j0 - p) - 256.0 * s).astype(np.float32)
                else:
                    beta[h, :, col] = (-s * (j0 + p - i0) + 256.0 * s).astype(np.float32)
        beta[h, :, 256] = 0.0
    coreints = np.zeros((1, 8), np.int32)
    coreints[0, 0] = my_base
    agmask = np.zeros((1, 64), np.float32)
    agmask[0, :8 * core] = 1.0
    kinds = np.zeros((5, SEQ), np.float64)
    kinds[0] = 1.0
    jj = np.arange(SEQ)
    for qt in range(NQT):
        kinds[1 + qt] = (jj < my_base + QW * qt).astype(np.float64)
    augq = np.zeros((5, NH * T), np.float64)
    a = np.arange(QW, dtype=np.float64) - 256.0
    for h in range(NH):
        s = SLOPES[h]
        for qt in range(NQT):
            colb = h * T + qt * QW
            augq[0, colb:colb + QW] = 8.0 * s * a
            augq[1 + qt, colb:colb + QW] = -16.0 * s * a
    return (beta, coreints, agmask,
            kinds.astype(ml_dtypes.bfloat16), augq.astype(ml_dtypes.bfloat16))


def _host_shared():
    p = np.arange(128, dtype=np.float64)[:, None]
    m = np.arange(896, dtype=np.float64)[None, :]
    strip = (-8.0 * np.abs(p - m + 384.0)).astype(np.float32)
    ident = np.eye(128).astype(ml_dtypes.bfloat16)
    eoff = np.zeros((128, 8), np.float32)
    eoff[:, :NRE] = (np.arange(NRE) * CAP)[None, :].astype(np.float32)
    onesrow = np.ones((1, 128), np.float32)
    onescol = np.ones((128, 1)).astype(ml_dtypes.bfloat16)
    sel2 = np.zeros((128, 2), np.float32)
    sel2[0:64, 0] = 1.0
    sel2[64:128, 1] = 1.0
    return strip, ident, eoff, onesrow, onescol, sel2.astype(ml_dtypes.bfloat16)


class _Prog:
    def __init__(self, nc):
        self.nc = nc
        self.ops = {e: [] for e in ENGS}
        self.cnt = {}
        self.waited = {e: {} for e in ENGS}
        self._sems = {}
        self._stack = ExitStack()

    def sem(self, name):
        s = self._stack.enter_context(self.nc.semaphore(name))
        self._sems[name] = s
        self.cnt[name] = 0

    def w(self, e, sem, val=None):
        if val is None:
            val = self.cnt[sem]
        if val <= 0 or self.waited[e].get(sem, 0) >= val:
            return
        self.waited[e][sem] = val
        s = self._sems[sem]
        self.ops[e].append(lambda eng, s=s, v=val: eng.wait_ge(s, v))

    def op(self, e, fn, inc=None, n=1):
        if inc is None:
            self.ops[e].append(fn)
            return None
        s = self._sems[inc]
        self.ops[e].append(lambda eng, fn=fn, s=s, n=n: fn(eng).then_inc(s, n))
        self.cnt[inc] += n
        return self.cnt[inc]

    def raw(self, e, fn):
        self.ops[e].append(fn)

    def emit(self):
        with self.nc.Block() as block:
            @block.sync
            def _(eng):
                for f in self.ops["sy"]:
                    f(eng)

            @block.gpsimd
            def _(eng):
                for f in self.ops["gp"]:
                    f(eng)

            @block.tensor
            def _(eng):
                for f in self.ops["pe"]:
                    f(eng)

            @block.vector
            def _(eng):
                for f in self.ops["dv"]:
                    f(eng)

            @block.scalar
            def _(eng):
                for f in self.ops["ac"]:
                    f(eng)

    def close(self):
        self._stack.close()


def _ap(t_or_ap, off, pattern):
    tensor = t_or_ap.tensor if isinstance(t_or_ap, bass.AP) else t_or_ap
    return bass.AP(tensor, off, pattern)


def build_program(taps=()):
    nc = bass.Bass("TRN2", target_bir_lowering=False, debug=False)
    taps = tuple(taps)

    def din(name, shape, dtype=F32):
        return nc.dram_tensor(name, shape, dtype, kind="ExternalInput")

    xT_d = din("xT", [D, T])
    wqkv_d = din("wqkv", [D, 3 * D])
    bqkv_d = din("bqkv", [3 * D, 1])
    wout_d = din("wout", [D, D])
    bout_d = din("bout", [D, 1])
    rwt_d = din("rwTh", [128, 64])
    rbt_d = din("rbtile", [128, 8])
    ltri_d = din("lt_tri", [128, 128])
    wi_d = din("my_wi", [D, DI])
    wo_d = din("my_wo", [DI, D])
    slab_d = din("betaslab", [NH, 128, 257])
    cints_d = din("coreints", [1, 8], I32)
    augq_d = din("augq", [5, NH * T], BF16)
    kinds_d = din("kinds", [5, SEQ], BF16)
    agm_d = din("agmask", [1, 64])
    stripm_d = din("stripm", [128, 896])
    ident_d = din("identbf", [128, 128], BF16)
    eoff_d = din("eofftile", [128, 8])
    onesr_d = din("onesrow", [1, 128])
    onesc_d = din("onescol", [128, 1], BF16)
    sel2_d = din("sel2", [128, 2], BF16)

    y_d = nc.dram_tensor("y", [T, D], F32, kind="ExternalOutput")

    kvag_in = nc.dram_tensor("kvag_in", [2 * D, T], BF16)
    kvag_out = nc.dram_tensor("kvag_out", [4 * 2 * D, T], BF16)
    knag_in = nc.dram_tensor("knag_in", [NH, 8], F32)
    knag_out = nc.dram_tensor("knag_out", [4 * NH, 8], F32)
    cntag_in = nc.dram_tensor("cntag_in", [8, 8], F32)
    cntag_out = nc.dram_tensor("cntag_out", [64, 8], F32)
    xtable = nc.dram_tensor("xtable", [TBLR, D], BF16)
    rsout_d = nc.dram_tensor("rs_out", [CAP, D], BF16)
    outag_in = nc.dram_tensor("outag_in", [CAP, D], BF16)
    outag_out = nc.dram_tensor("outag_out", [TBLR + 1, D], BF16)
    x1spill = nc.dram_tensor("x1spill", [T, D], F32)
    scr_d = nc.dram_tensor("scratch", [64, 8], F32)
    scr2_d = nc.dram_tensor("scratch2", [32, 2], F32)

    tap_t = {}
    def mktap(name, shape, dtype):
        if name in taps:
            tap_t[name] = nc.dram_tensor("tap_" + name, shape, dtype,
                                         kind="ExternalOutput")
    mktap("qall", [64, 8 * T], F32)
    mktap("oall", [128, 8 * T], F32)
    mktap("zf", [128, 8 * T], F32)
    mktap("x1", [128, 8 * T], F32)
    mktap("ltok", [128, 16 * 8], F32)
    mktap("gates", [128, 16 * 8], F32)
    mktap("grtok", [128, 16 * 8], F32)
    mktap("offs", [128, 32], F32)
    mktap("brow", [1, NH], F32)
    mktap("devt0", [128, 2 * 257], F32)
    mktap("ev30", [65, QW], F32)
    mktap("tstr0", [128, QW], F32)
    mktap("vstr0", [128, 65 * NKB], F32)
    mktap("murows", [4, QW], F32)
    mktap("xtb", [256, D], F32)
    mktap("x1sp", [256, D], F32)
    mktap("stg1", [128, D], F32)
    mktap("g1", [128, 2], F32)
    mktap("x4t", [128, D], F32)
    mktap("x1sp4", [128, D], F32)
    mktap("st32", [128, D], F32)
    mktap("zft", [128, 128], F32)
    mktap("idf", [128, 128], F32)
    mktap("pstr0", [128, 512], F32)
    mktap("rsx", [256, D], F32)
    mktap("varrows", [4, QW], F32)
    mktap("vostr0", [128, 65 * 16], F32)
    mktap("pts0", [128, QW], F32)
    mktap("ptm0", [128, QW], F32)
    mktap("hq", [128, 8 * 1024], F32)
    mktap("eout", [256, D], F32)
    mktap("wib", [128, 2048], F32)
    mktap("xgt", [128, 1024], F32)
    mktap("wib2", [128, 2048], F32)
    mktap("rsx2", [256, D], F32)

    p = _Prog(nc)
    for s in ("ld", "st", "gd", "pe", "dv", "ac", "gp"):
        p.sem(s)

    def dv(fn):
        p.w("dv", "dv")
        return p.op("dv", fn, inc="dv")

    def ac(fn):
        p.w("ac", "ac")
        return p.op("ac", fn, inc="ac")

    def gpc(fn):
        p.w("gp", "gp")
        return p.op("gp", fn, inc="gp")

    def pe(fn):
        return p.op("pe", fn)

    def pei(fn):
        return p.op("pe", fn, inc="pe")

    def load(dst, src, eng="sy", slow=False):
        sem = "ld" if eng == "sy" else "gd"
        if slow:
            fn = lambda e, d=dst, s=src: e.dma_start(
                out=d, in_=s, allow_slow_non_contiguous=True)
        else:
            fn = lambda e, d=dst, s=src: e.dma_start(out=d, in_=s)
        return p.op(eng, fn, inc=sem, n=16)

    def store(dst, src, eng="sy", slow=False):
        sem = "st" if eng == "sy" else "gd"
        if slow:
            fn = lambda e, d=dst, s=src: e.dma_start(
                out=d, in_=s, allow_slow_non_contiguous=True)
        else:
            fn = lambda e, d=dst, s=src: e.dma_start(out=d, in_=s)
        return p.op(eng, fn, inc=sem, n=16)

    S0 = ExitStack()
    sb0 = lambda n, s, d: S0.enter_context(nc.sbuf_tensor(n, s, d))
    identb = sb0("identb", [128, 128], BF16)
    identf = sb0("identf", [128, 128], F32)
    stripm = sb0("stripm_s", [128, 896], F32)
    onesrow = sb0("onesrow_s", [1, 128], F32)
    onescol = sb0("onescol_s", [128, 1], BF16)
    sel2 = sb0("sel2_s", [128, 2], BF16)
    eoff = sb0("eoff_s", [128, 8], F32)
    cints = sb0("cints", [1, 8], I32)
    agm = sb0("agm", [1, 64], F32)
    b128 = sb0("b128", [128, NH], F32)
    brow = sb0("brow_s", [1, NH], F32)
    offi = sb0("offi", [128, 32], I32)
    npsc = sb0("npsc", [128, 16], F32)
    eps128 = sb0("eps128", [128, 1], F32)
    onescol128f = sb0("onescol128f", [128, 1], F32)
    zrowb = sb0("zrowb", [1, QW], BF16)
    zcolb = sb0("zcolb", [1, 65], BF16)
    zrows = sb0("zrows", [128, D], BF16)
    zrow32 = sb0("zrow32", [1, D], F32)

    for dst, src in ((identb[:], ident_d[:, :]),
                     (stripm[:], stripm_d[:, :]), (onesrow[:], onesr_d[:, :]),
                     (onescol[:], onesc_d[:, :]), (sel2[:], sel2_d[:, :]),
                     (eoff[:], eoff_d[:, :]),
                     (cints[:], cints_d[:, :]), (agm[:], agm_d[:, :])):
        load(dst, src)
    ld_const = p.cnt["ld"]
    p.w("dv", "ld", ld_const)
    dv(lambda e: e.tensor_copy(identf[:], identb[:]))
    dv(lambda e: e.memset(eps128[:], 1e-5))
    dv(lambda e: e.memset(onescol128f[:], 1.0))
    dv(lambda e: e.memset(zrowb[:], 0.0))
    dv(lambda e: e.memset(zcolb[:], 0.0))
    dv(lambda e: e.memset(zrows[:], 0.0))
    dv(lambda e: e.memset(zrow32[:], 0.0))
    zdone = p.cnt["dv"]
    p.w("gp", "dv", zdone)
    store(outag_out[DUMP_G:DUMP_G + 1, :], zrows[0:1, :], eng="gp")

    S_o = ExitStack()
    oall = S_o.enter_context(nc.sbuf_tensor("oall", [128, 8 * T], BF16))
    S1q = ExitStack()

    qaug = S1q.enter_context(nc.sbuf_tensor("qaug", [69, NH * T], BF16))
    S1 = ExitStack()
    sb1 = lambda n, s, d: S1.enter_context(nc.sbuf_tensor(n, s, d))
    xb = sb1("xb", [128, 8 * T], BF16)
    wb = sb1("wb", [128, 8 * 3 * D], BF16)
    bia = sb1("bia", [128, 24], F32)
    stg = sb1("stg", [128, 2 * T], BF16)
    sqt = sb1("sqt", [128, T], BF16)
    norm2k = sb1("norm2k", [2, 8], F32)
    qrow = sb1("qrow", [1, NH], F32)
    qn = sb1("qn", [NH, 1], F32)
    kn = sb1("kn", [NH, 1], F32)
    knago = sb1("knago", [NH, 4], F32)
    kin = sb1("kin", [NH, 8], F32)
    kmax2 = sb1("kmax2", [NH, 1], F32)
    prod = sb1("prod", [NH, 1], F32)
    bcol = sb1("bcol", [NH, 1], F32)
    ps_qkv = S1.enter_context(nc.psum_tensor("ps_qkv", [128, 2 * QW], F32))
    ps_n = S1.enter_context(nc.psum_tensor("ps_n", [2, T], F32))
    ps_b = S1.enter_context(nc.psum_tensor("ps_b", [128, NH], F32))

    for c in range(8):
        load(xb[:, c * T:(c + 1) * T], xT_d[128 * c:128 * (c + 1), :], eng="gp")
    for c in range(8):
        load(wb[:, c * 3 * D:(c + 1) * 3 * D],
             wqkv_d[128 * c:128 * (c + 1), :], eng="gp")
    gd_in = p.cnt["gd"]
    for j in range(24):
        load(bia[:, j:j + 1], bqkv_d[128 * j:128 * (j + 1), :])
    ld_b = p.cnt["ld"]
    ld_augq = load(qaug[64:69, :], augq_d[:, :])
    dv(lambda e: e.memset(norm2k[:], 0.0))
    nzero = p.cnt["dv"]

    p.w("pe", "gd", gd_in)
    p.w("ac", "ld", ld_b)

    ev_hist = {}
    st_hist = {}
    for t in range(24):
        is_q = t < 8
        for n in range(4):
            bank = (t * 4 + n) % 2
            pst = ps_qkv[:, bank * QW:(bank + 1) * QW]
            prev = ev_hist.get(bank)
            if prev is not None:
                p.w("pe", "ac", prev)
            for kc in range(8):
                f = (lambda e, pst=pst, kc=kc, t=t, n=n:
                     e.matmul(pst,
                              wb[:, kc * 3 * D + 128 * t: kc * 3 * D + 128 * (t + 1)],
                              xb[:, kc * T + n * QW: kc * T + (n + 1) * QW],
                              start=(kc == 0), stop=(kc == 7)))
                pei(f) if kc == 7 else pe(f)
            mmc = p.cnt["pe"]
            p.w("ac", "pe", mmc)
            if is_q:
                ac(lambda e, pst=pst, t=t, n=n: e.activation(
                    qaug[0:64, (2 * t) * T + n * QW: (2 * t) * T + (n + 1) * QW],
                    pst[0:64, :], AF.Identity, bias=bia[0:64, t:t + 1], scale=1.0))
                ev = ac(lambda e, pst=pst, t=t, n=n: e.activation(
                    qaug[0:64, (2 * t + 1) * T + n * QW:
                         (2 * t + 1) * T + (n + 1) * QW],
                    pst[64:128, :], AF.Identity, bias=bia[64:128, t:t + 1],
                    scale=1.0))
            else:
                half = t % 2
                prevst = st_hist.get(half)
                if prevst is not None:
                    p.w("ac", "st", prevst)
                ev = ac(lambda e, pst=pst, t=t, n=n, half=half:
                        e.activation(stg[:, half * T + n * QW: half * T + (n + 1) * QW],
                                     pst, AF.Identity, bias=bia[:, t:t + 1],
                                     scale=1.0))
            ev_hist[bank] = ev
        chunk_done = p.cnt["ac"]
        if not is_q:
            if t < 16:
                src_ = stg[:, (t % 2) * T:(t % 2 + 1) * T]
                p.w("dv", "dv", nzero)
                sv = ac(lambda e, s=src_: e.activation(sqt[:], s, AF.Square))
                p.w("pe", "ac", sv)
                for n in range(4):
                    f = (lambda e, n=n: e.matmul(
                        ps_n[:, n * QW:(n + 1) * QW], sel2[:],
                        sqt[:, n * QW:(n + 1) * QW], start=True, stop=True))
                    pei(f) if n == 3 else pe(f)
                p.w("dv", "pe")
                rmax = dv(lambda e, t=t: e.tensor_reduce(
                    norm2k[:, (t - 8):(t - 8) + 1], ps_n[:], AX, OP.max))
                p.w("ac", "dv", rmax)
                p.w("pe", "dv", rmax)
            rows = 128 * (t - 8)
            p.w("sy", "ac", chunk_done)
            stc = store(kvag_in[rows:rows + 128, :],
                        stg[:, (t % 2) * T:(t % 2 + 1) * T])
            st_hist[t % 2] = stc

    p.w("ac", "ac")
    for h in range(NH):
        sv = ac(lambda e, h=h: e.activation(sqt[0:64, :], qaug[0:64, h * T:(h + 1) * T],
                                            AF.Square))
        p.w("pe", "ac", sv)
        for n in range(4):
            f = (lambda e, n=n: e.matmul(
                ps_n[0:1, n * QW:(n + 1) * QW], onescol[0:64, :],
                sqt[0:64, n * QW:(n + 1) * QW], start=True, stop=True))
            pei(f) if n == 3 else pe(f)
        p.w("dv", "pe")
        rq = dv(lambda e, h=h: e.tensor_reduce(qrow[:, h:h + 1],
                                               ps_n[0:1, :], AX, OP.max))
        p.w("ac", "dv", rq)
        p.w("pe", "dv", rq)
    qn_done = p.cnt["dv"]

    if "qall" in tap_t:
        p.w("gp", "ac")
        store(tap_t["qall"].ap(), qaug[0:64, 0:8 * T], eng="gp")

    p.w("gp", "st")
    p.w("gp", "gd")
    cc_kv = gpc(lambda e: e.collective_compute(
        "AllGather", OP.bypass, replica_groups=[[0, 1, 2, 3], [4, 5, 6, 7]],
        ins=[kvag_in.ap()], outs=[kvag_out.ap()]))

    p.w("sy", "dv")
    st_n = store(scr2_d.ap()[0:8, 0:2].rearrange("a b -> b a"), norm2k[:],
                 slow=True)
    st_q = store(scr2_d.ap()[8:16, 0:2].rearrange("a b -> (a b)")
                 .rearrange("(a b) -> a b", a=1), qrow[:])
    scr2_flat = scr2_d.ap().rearrange("a b -> (a b)")
    p.w("sy", "st", st_q)
    ld1 = load(qn[:], scr2_flat[16:32].rearrange("(a b) -> a b", b=1))
    ld2 = load(kn[:], scr2_flat[0:16].rearrange("(a b) -> a b", b=1))
    dv(lambda e: e.memset(kin[:], 0.0))
    p.w("dv", "ld", ld2)
    dv(lambda e: e.tensor_copy(kin[:, 0:1], kn[:]))
    p.w("gp", "dv")
    store(knag_in.ap(), kin[:], eng="gp")
    p.w("gp", "gd")
    cc_kn = gpc(lambda e: e.collective_compute(
        "AllGather", OP.bypass, replica_groups=[[0, 1, 2, 3], [4, 5, 6, 7]],
        ins=[knag_in.ap()], outs=[knag_out.ap()]))
    p.w("sy", "gp", cc_kn)
    ld3 = load(knago[:], _ap(knag_out.ap(), 0, [[8, 16], [128, 4]]), slow=True)
    p.w("dv", "ld", ld3)
    dv(lambda e: e.tensor_reduce(kmax2[:], knago[:], AX, OP.max))
    p.w("dv", "ld", ld1)
    dv(lambda e: e.tensor_mul(prod[:], qn[:], kmax2[:]))
    pc = p.cnt["dv"]
    p.w("ac", "dv", pc)
    bc = ac(lambda e: e.activation(bcol[:], prod[:], AF.Sqrt,
                                   scale=(1.02 / 8.0) ** 2))
    p.w("sy", "ac", bc)
    st_b = store(scr_d.ap()[0:16, 0:1], bcol[:], slow=True)
    p.w("sy", "st", st_b)
    ld4 = load(brow[:], _ap(scr_d.ap(), 0, [[512, 1], [8, 16]]), slow=True)
    if "brow" in tap_t:
        p.w("sy", "ld", ld4)
        store(tap_t["brow"].ap(), brow[:])
    p.w("pe", "ld", ld4)
    pei(lambda e: e.matmul(ps_b[:], onesrow[:], brow[:], start=True, stop=True))
    p.w("dv", "pe")
    dv(lambda e: e.tensor_copy(b128[:], ps_b[:]))
    b_done = p.cnt["dv"]
    S1.close()

    S2 = ExitStack()
    sb2 = lambda n, s, d: S2.enter_context(nc.sbuf_tensor(n, s, d))
    kaug = [sb2("kaug0", [69, SEQ], BF16), sb2("kaug1", [69, SEQ], BF16)]
    kown = [sb2("kown0", [64, T], BF16), sb2("kown1", [64, T], BF16)]
    vfm = sb2("vfm", [128, SEQ], BF16)
    vstr = [sb2("vstr0", [128, 65 * NKB], BF16),
            sb2("vstr1", [128, 65 * NKB], BF16)]
    vostr = [sb2("vostr0", [128, 65 * 16], BF16),
             sb2("vostr1", [128, 65 * 16], BF16)]
    devt = sb2("devt", [128, 2 * 257], F32)
    slabb = sb2("slabb", [128, 2 * 257], F32)
    strips = sb2("strips", [128, 2 * 896], BF16)
    ptb = sb2("ptb", [128, 2 * QW], BF16)
    tstr = sb2("tstr", [128, 2 * QW], F32)
    ev3 = sb2("ev3", [65, QW], F32)
    rrow = sb2("rrow", [1, QW], F32)
    ps_s = S2.enter_context(nc.psum_tensor("ps_s", [128, 2 * QW], F32))
    ps_acc = S2.enter_context(nc.psum_tensor("ps_acc", [65, QW], F32))
    ps_acc2 = S2.enter_context(nc.psum_tensor("ps_acc2", [65, QW], F32))
    psacs = [ps_acc, ps_acc2]
    ps_vt = S2.enter_context(nc.psum_tensor("ps_vt", [128, 1024], BF16))
    ps_r = S2.enter_context(nc.psum_tensor("ps_r", [64, QW], F32))

    p.w("gp", "dv", zdone)
    for i in range(TBLR // 128):
        store(xtable[128 * i:128 * (i + 1), :], zrows[:], eng="gp")
    gd_zero = p.cnt["gd"]

    exp_hist = {}
    texp_hist = {}
    pv_hist = {}
    fin = None
    tile_ctr = [0]
    rc_hist = {}
    pend_fin = [None]
    vtr_done = None
    for ph in range(8):
        p.w("sy", "gp", cc_kv)
        if vtr_done is not None:
            p.w("sy", "pe", vtr_done)
        for r in range(4):
            load(vfm[:, r * T:(r + 1) * T],
                 kvag_out[r * 2 * D + D + 128 * ph:
                          r * 2 * D + D + 128 * (ph + 1), :])
        ld_vfm = p.cnt["ld"]
        p.w("sy", "pe")
        p.w("sy", "ac")
        for hh in (0, 1):
            for r in range(4):
                load(kaug[hh][0:64, r * T:(r + 1) * T],
                     kvag_out[r * 2 * D + 128 * ph + 64 * hh:
                              r * 2 * D + 128 * ph + 64 * hh + 64, :])
            load(kaug[hh][64:69, :], kinds_d[:, :])
            load(kown[hh][:, :],
                 kvag_in[128 * ph + 64 * hh: 128 * ph + 64 * hh + 64, :])
        ld_kv = p.cnt["ld"]
        load(slabb[:, 0:257], slab_d[2 * ph])
        load(slabb[:, 257:514], slab_d[2 * ph + 1])
        ld_slab = p.cnt["ld"]

        p.w("dv", "ld", ld_slab)
        p.w("dv", "dv", b_done)
        p.w("dv", "ac")
        for hh in (0, 1):
            dv(lambda e, hh=hh, ph=ph: e.tensor_scalar(
                devt[:, hh * 257:(hh + 1) * 257],
                slabb[:, hh * 257:(hh + 1) * 257],
                b128[:, 2 * ph + hh:2 * ph + hh + 1], None, OP.subtract))
            dv(lambda e, hh=hh, ph=ph: e.tensor_scalar(
                strips[:, hh * 896:(hh + 1) * 896], stripm[:],
                float(SLOPES[2 * ph + hh]), None, OP.mult))
        tabs_done = p.cnt["dv"]
        if "devt0" in tap_t and ph == 7:
            p.w("sy", "dv", tabs_done)
            store(tap_t["devt0"].ap(), devt[:])

        dv(lambda e: e.memset(vstr[0][:], 1.0))
        dv(lambda e: e.memset(vstr[1][:], 1.0))
        dv(lambda e: e.memset(vostr[0][:], 1.0))
        dv(lambda e: e.memset(vostr[1][:], 1.0))
        msdone = p.cnt["dv"]
        p.w("pe", "ld", ld_vfm)
        p.w("pe", "dv", msdone)
        for g in range(8):
            for m in range(8):
                j = 8 * g + m
                f = (lambda e, j=j, m=m: e.transpose(
                    ps_vt[:, 128 * m:128 * (m + 1)],
                    vfm[:, 128 * j:128 * (j + 1)], identb[:]))
                pei(f) if m == 7 else pe(f)
            tp = p.cnt["pe"]
            p.w("dv", "pe", tp)
            for hh in (0, 1):
                src = _ap(ps_vt, 64 * hh, [[1024, 128], [128, 8], [1, 64]])
                dst = _ap(vstr[hh], 65 * 8 * g, [[65 * NKB, 128], [65, 8], [1, 64]])
                dv(lambda e, s=src, d=dst: e.tensor_copy(d, s))
            evv = p.cnt["dv"]
            p.w("pe", "dv", evv)
        vof = sb2("vof", [128, T], BF16) if ph == 0 else vof
        for hh in (0, 1):
            p.w("sy", "pe")
            lvo = load(vof[0:64, :],
                       kvag_in[D + 128 * ph + 64 * hh: D + 128 * ph + 64 * hh + 64, :])
            p.w("pe", "ld", lvo)
            p.w("pe", "dv")
            for m in range(16):
                f = (lambda e, m=m: e.transpose(
                    ps_vt[0:128, 64 * (m % 8):64 * (m % 8) + 64],
                    vof[0:64, 128 * m:128 * (m + 1)], identb[0:64, 0:64]))
                pei(f) if (m % 8 == 7) else pe(f)
                if m % 8 == 7:
                    tp = p.cnt["pe"]
                    p.w("dv", "pe", tp)
                    g0 = m // 8
                    src = _ap(ps_vt, 0, [[1024, 128], [64, 8], [1, 64]])
                    dst = _ap(vostr[hh], 65 * 8 * g0,
                              [[65 * 16, 128], [65, 8], [1, 64]])
                    dv(lambda e, s=src, d=dst: e.tensor_copy(d, s))
                    evv = p.cnt["dv"]
                    p.w("pe", "dv", evv)
        vdone = p.cnt["dv"]
        vtr_done = p.cnt["pe"]
        if ph == 7 and "vstr0" in tap_t:
            p.w("gp", "dv", vdone)
            store(tap_t["vstr0"].ap(), vstr[0][:], eng="gp")
            store(tap_t["vostr0"].ap(), vostr[0][:], eng="gp")

        for hh in (0, 1):
            for qt in range(4):
                buf = tile_ctr[0] % 2
                tile_ctr[0] += 1
                psa = psacs[buf]
                p.w("pe", "dv", vdone)
                p.w("pe", "dv", tabs_done)
                p.w("pe", "ld", ld_kv)
                prc = rc_hist.get(buf)
                if prc is not None:
                    p.w("pe", "dv", prc)
                pe(lambda e, psa=psa: e.matmul(psa[:, :], zcolb[:], zrowb[:],
                                               start=True, stop=False,
                                               skip_group_check=True))

                nblk = NKB + 4

                def issue_smm(bk, pos, hh=hh, qt=qt, ph=ph):
                    strad = bk >= NKB
                    dd = bk - NKB
                    bsl = pos % 2
                    spt = ps_s[:, bsl * QW:(bsl + 1) * QW]
                    prev = exp_hist.get(bsl)
                    if prev is not None:
                        p.w("pe", "ac", prev)
                    if not strad:
                        pei(lambda e, spt=spt, bk=bk:
                            e.matmul(spt,
                                     kaug[hh][:, 128 * bk:128 * (bk + 1)],
                                     qaug[:, (2 * ph + hh) * T + qt * QW:
                                          (2 * ph + hh) * T + (qt + 1) * QW],
                                     start=True, stop=True))
                    else:
                        pei(lambda e, spt=spt, dd=dd:
                            e.matmul(spt,
                                     kown[hh][:, QW * qt + 128 * dd:
                                              QW * qt + 128 * (dd + 1)],
                                     qaug[0:64, (2 * ph + hh) * T + qt * QW:
                                          (2 * ph + hh) * T + (qt + 1) * QW],
                                     start=True, stop=True))
                    return p.cnt["pe"]

                order = list(range(NKB, NKB + 4)) + list(range(NKB))
                smm_q = [issue_smm(order[0], 0)]
                for i in range(nblk):
                    bk = order[i]
                    strad = bk >= NKB
                    dd = bk - NKB
                    bsl = i % 2
                    spt = ps_s[:, bsl * QW:(bsl + 1) * QW]
                    col = (qt * NKB + bk) if not strad else 256
                    smmc = smm_q.pop(0)
                    if i + 1 < nblk:
                        smm_q.append(issue_smm(order[i + 1], i + 1))
                    if i == 2 and pend_fin[0] is not None:
                        pend_fin[0]()
                        pend_fin[0] = None
                    if strad:
                        p.w("dv", "pe", smmc)
                        prevexp = exp_hist.get(bsl)
                        if prevexp is not None:
                            p.w("dv", "ac", prevexp)
                        soff = 896 * hh + 384 - 128 * dd
                        tsl = tstr[:, bsl * QW:(bsl + 1) * QW]
                        tdv = dv(lambda e, spt=spt, soff=soff, tsl=tsl:
                                 e.tensor_tensor(tsl, spt,
                                                 strips[:, soff:soff + QW],
                                                 OP.add))
                        p.w("ac", "dv", tdv)
                        expsrc = tsl
                    else:
                        expsrc = spt
                    p.w("ac", "pe", smmc)
                    prevpv = pv_hist.get(bsl)
                    if prevpv is not None:
                        p.w("ac", "pe", prevpv)
                    expc = ac(lambda e, src=expsrc, bsl=bsl, hh=hh, col=col:
                              e.activation(ptb[:, bsl * QW:(bsl + 1) * QW], src,
                                           AF.Exp,
                                           bias=devt[:, hh * 257 + col:
                                                     hh * 257 + col + 1],
                                           scale=0.125))
                    exp_hist[bsl] = expc
                    if ph == 7 and hh == 0 and qt == 0:
                        if strad and dd == 0 and "tstr0" in tap_t:
                            p.w("gp", "dv", tdv)
                            store(tap_t["tstr0"].ap(), tstr[:, 0:QW], eng="gp")
                            p.w("gp", "ac", expc)
                            store(tap_t["pts0"].ap(),
                                  ptb[:, bsl * QW:(bsl + 1) * QW], eng="gp")
                            p.w("pe", "gd")
                        if (not strad) and i == 12 and "ptm0" in tap_t:
                            p.w("gp", "ac", expc)
                            store(tap_t["ptm0"].ap(),
                                  ptb[:, bsl * QW:(bsl + 1) * QW], eng="gp")
                            p.w("pe", "gd")
                    p.w("pe", "ac", expc)
                    if not strad:
                        pei(lambda e, psa=psa, hh=hh, bk=bk, bsl=bsl: e.matmul(
                            psa[:, :], vstr[hh][:, 65 * bk:65 * bk + 65],
                            ptb[:, bsl * QW:(bsl + 1) * QW],
                            start=False, stop=False, skip_group_check=True))
                    else:
                        pei(lambda e, psa=psa, hh=hh, qt=qt, dd=dd, bsl=bsl:
                            e.matmul(
                            psa[:, :],
                            vostr[hh][:, 65 * (4 * qt + dd):
                                      65 * (4 * qt + dd) + 65],
                            ptb[:, bsl * QW:(bsl + 1) * QW],
                            start=False, stop=False, skip_group_check=True))
                    pv_hist[bsl] = p.cnt["pe"]

                pvall = p.cnt["pe"]
                p.w("dv", "pe", pvall)
                dv(lambda e, psa=psa: e.tensor_copy(ev3[:], psa[:, :]))
                dv(lambda e: e.reciprocal(rrow[:], ev3[64:65, :]))
                rc = p.cnt["dv"]
                rc_hist[buf] = rc
                if "ev30" in tap_t and ph == 7 and hh == 0 and qt == 0:
                    p.w("sy", "dv", rc)
                    store(tap_t["ev30"].ap(), ev3[:])
                dst = oall[64 * hh:64 * (hh + 1),
                           ph * T + qt * QW: ph * T + (qt + 1) * QW]

                def mk_fin(rc=rc, dst=dst):
                    def flush():
                        global fin
                        p.w("pe", "dv", rc)
                        pei(lambda e: e.matmul(ps_r[:, :], onesrow[:, 0:64],
                                               rrow[:], start=True, stop=True))
                        bcr = p.cnt["pe"]
                        p.w("dv", "pe", bcr)
                        dv(lambda e, d=dst: e.tensor_mul(d, ev3[0:64, :],
                                                         ps_r[:, :]))
                    return flush
                pend_fin[0] = mk_fin()

    if pend_fin[0] is not None:
        pend_fin[0]()
        pend_fin[0] = None
    if "oall" in tap_t:
        p.w("gp", "dv")
        store(tap_t["oall"].ap(), oall[:], eng="gp")
    S2.close()
    S1q.close()

    S3 = ExitStack()
    x1b = S3.enter_context(nc.sbuf_tensor("x1b", [128, 8 * T], BF16))

    S3z = ExitStack()
    sb3 = lambda n, s, d: S3z.enter_context(nc.sbuf_tensor(n, s, d))
    zf = sb3("zf", [128, 8 * T], F32)
    wob = sb3("wob", [128, 8 * D], BF16)
    boutc = sb3("boutc", [128, 8], F32)
    bvf = sb3("bvf", [128, 8], F32)
    bvb = sb3("bvb", [128, 8], BF16)
    pbias = sb3("pbias", [128, 8], F32)
    xfc = sb3("xfc", [128, 2 * 1024], F32)
    zb2 = sb3("zb2", [128, 2 * QW], BF16)
    zq2 = sb3("zq2", [128, 2 * QW], BF16)
    murow = sb3("murow", [1, QW], F32)
    varrow = sb3("varrow", [1, QW], F32)
    rsrow = sb3("rsrow", [1, QW], F32)
    tmpq = sb3("tmpq", [128, QW], F32)
    stg32 = sb3("stg32", [128, D], F32)
    ps_p = S3z.enter_context(nc.psum_tensor("ps_p", [128, 2 * QW], F32))
    ps_cm = S3z.enter_context(nc.psum_tensor("ps_cm", [128, QW], F32))
    ps_c = ps_cm
    ps_sq = S3z.enter_context(nc.psum_tensor("ps_sq", [1, QW], F32))
    ps_mb = S3z.enter_context(nc.psum_tensor("ps_mb", [128, QW], F32))
    ps_rb = S3z.enter_context(nc.psum_tensor("ps_rb", [128, QW], F32))
    ps_trA = S3z.enter_context(nc.psum_tensor("ps_trA", [128, QW], F32))
    ps_trB = S3z.enter_context(nc.psum_tensor("ps_trB", [128, QW], F32))
    ps_trh = [ps_trA, ps_trB]

    for _eng in ("sy", "gp"):
        p.w(_eng, "pe")
        p.w(_eng, "ac")
        p.w(_eng, "dv")
    for c in range(8):
        load(wob[:, c * D:(c + 1) * D], wout_d[128 * c:128 * (c + 1), :],
             eng="gp")
    gd_w = p.cnt["gd"]
    for j in range(8):
        load(boutc[:, j:j + 1], bout_d[128 * j:128 * (j + 1), :])
    for j in range(8):
        load(bvf[:, j:j + 1], bqkv_d[2 * D + 128 * j: 2 * D + 128 * (j + 1), :])
    ld_pb = p.cnt["ld"]
    p.w("dv", "ld", ld_pb)
    dv(lambda e: e.tensor_copy(bvb[:], bvf[:]))
    cvt = p.cnt["dv"]
    p.w("pe", "dv", cvt)
    p.w("pe", "gd", gd_w)
    dv(lambda e: e.tensor_copy(pbias[:], boutc[:]))
    pbias_done = p.cnt["dv"]

    p.w("pe", "dv")
    evh = {}
    for t in range(8):
        for n in range(4):
            bank = (t * 4 + n) % 2
            pst = ps_p[:, bank * QW:(bank + 1) * QW]
            prev = evh.get(bank)
            if prev is not None:
                p.w("pe", "ac", prev)
            for kc in range(8):
                f = (lambda e, t=t, n=n, kc=kc, pst=pst: e.matmul(
                    pst, wob[:, kc * D + 128 * t: kc * D + 128 * (t + 1)],
                    oall[:, kc * T + n * QW: kc * T + (n + 1) * QW],
                    start=(kc == 0), stop=(kc == 7)))
                pei(f) if kc == 7 else pe(f)
            mmc = p.cnt["pe"]
            p.w("ac", "pe", mmc)
            p.w("ac", "dv", pbias_done)
            ev = ac(lambda e, pst=pst, t=t, n=n: e.activation(
                zf[:, t * T + n * QW: t * T + (n + 1) * QW], pst, AF.Identity,
                bias=pbias[:, t:t + 1], scale=1.0))
            evh[bank] = ev
    proj_done = p.cnt["ac"]

    lx_hist = {}
    for ch in range(16):
        c, hh2 = ch // 2, ch % 2
        half = ch % 2
        prev_add = lx_hist.get(("add", half))
        if prev_add is not None:
            p.w("sy", "dv", prev_add)
        lx = load(xfc[:, half * 1024:(half + 1) * 1024],
                  xT_d[128 * c:128 * (c + 1), 1024 * hh2:1024 * (hh2 + 1)])
        p.w("dv", "ld", lx)
        p.w("dv", "ac", proj_done)
        a = dv(lambda e, c=c, hh2=hh2, half=half: e.tensor_add(
            zf[:, c * T + 1024 * hh2: c * T + 1024 * (hh2 + 1)],
            zf[:, c * T + 1024 * hh2: c * T + 1024 * (hh2 + 1)],
            xfc[:, half * 1024:(half + 1) * 1024]))
        lx_hist[("add", half)] = a
    zready = p.cnt["dv"]
    if "zf" in tap_t:
        p.w("sy", "dv", zready)
        store(tap_t["zf"].ap(), zf[:])

    for n in range(4):
        for t in range(8):
            half = t % 2
            p.w("dv", "dv", zready)
            cb = dv(lambda e, t=t, n=n, half=half: e.tensor_copy(
                zb2[:, half * QW:(half + 1) * QW],
                zf[:, t * T + n * QW: t * T + (n + 1) * QW]))
            p.w("pe", "dv", cb)
            f = (lambda e, t=t, half=half: e.matmul(
                ps_cm[0:1, :], onescol[:], zb2[:, half * QW:(half + 1) * QW],
                start=(t == 0), stop=(t == 7)))
            pei(f) if t == 7 else pe(f)
            smc = p.cnt["pe"]
            p.w("dv", "pe", smc)
        p.w("ac", "dv", zready)
        for t in range(8):
            half = t % 2
            sb_ = ac(lambda e, t=t, n=n, half=half: e.activation(
                zq2[:, half * QW:(half + 1) * QW],
                zf[:, t * T + n * QW: t * T + (n + 1) * QW], AF.Square))
            p.w("pe", "ac", sb_)
            f = (lambda e, t=t, half=half: e.matmul(
                ps_sq[:, :], onescol[:], zq2[:, half * QW:(half + 1) * QW],
                start=(t == 0), stop=(t == 7)))
            pei(f) if t == 7 else pe(f)
            smc = p.cnt["pe"]
            p.w("ac", "pe", smc)
        smc = p.cnt["pe"]
        p.w("dv", "pe", smc)
        dv(lambda e: e.tensor_scalar(murow[:], ps_cm[0:1, :], 1.0 / D, None,
                                     OP.mult))
        dv(lambda e: e.tensor_scalar(varrow[:], ps_sq[:, :], 1.0 / D, None,
                                     OP.mult))
        dv(lambda e: e.tensor_mul(rsrow[:], murow[:], murow[:]))
        dv(lambda e: e.tensor_sub(varrow[:], varrow[:], rsrow[:]))
        vd = p.cnt["dv"]
        if "murows" in tap_t:
            p.w("sy", "dv", vd)
            store(tap_t["murows"].ap()[n:n + 1, :], murow[:])
            store(tap_t["varrows"].ap()[n:n + 1, :], varrow[:])
            p.w("dv", "st")
        p.w("pe", "dv", vd)
        p.w("ac", "dv", vd)
        sdv = ac(lambda e: e.activation(rsrow[:], varrow[:], AF.Sqrt, bias=eps128[0:1, 0:1]))
        p.w("dv", "ac", sdv)
        dv(lambda e: e.reciprocal(rsrow[:], rsrow[:]))
        rowsdone = p.cnt["dv"]
        p.w("pe", "dv", rowsdone)
        pe(lambda e, n=n: e.matmul(ps_mb[:, :], onesrow[:], murow[:],
                                   start=True, stop=True))
        pei(lambda e, n=n: e.matmul(ps_rb[:, :], onesrow[:], rsrow[:],
                                    start=True, stop=True))
        bcc = p.cnt["pe"]
        p.w("dv", "pe", bcc)
        for t in range(8):
            dv(lambda e, t=t, n=n: e.tensor_sub(
                tmpq[:], zf[:, t * T + n * QW: t * T + (n + 1) * QW],
                ps_mb[:, :]))
            dv(lambda e, t=t, n=n: e.tensor_mul(
                zf[:, t * T + n * QW: t * T + (n + 1) * QW], tmpq[:],
                ps_rb[:, :]))
        lnc_ = p.cnt["dv"]
        p.w("pe", "dv", lnc_)
    x1done = p.cnt["dv"]
    if "x1" in tap_t:
        p.w("sy", "dv", x1done)
        store(tap_t["x1"].ap(), zf[:])

    for c in range(8):
        dv(lambda e, c=c: e.tensor_copy(x1b[:, c * T:(c + 1) * T],
                                        zf[:, c * T:(c + 1) * T]))
    x1b_done = p.cnt["dv"]

    if "zft" in tap_t:
        p.w("sy", "dv", x1done)
        store(tap_t["zft"].ap(), zf[:, 0:128])
    if "idf" in tap_t:
        store(tap_t["idf"].ap(), identf[:])
    tspare = None
    if "pstr0" in tap_t:
        tspare = S3z.enter_context(nc.sbuf_tensor("tspare", [128, 512], F32))
    p.w("pe", "dv", x1done)
    st_x1 = None
    evc = None
    for tt in range(16):
        if st_x1 is not None:
            p.w("pe", "st", st_x1)
        for half in range(2):
            for q in range(4):
                c = half * 4 + q
                f = (lambda e, tt=tt, c=c, half=half, q=q: e.transpose(
                    ps_trh[half][:, 128 * q:128 * (q + 1)],
                    zf[:, c * T + 128 * tt: c * T + 128 * (tt + 1)], identf[:]))
                pei(f) if q == 3 else pe(f)
            trc = p.cnt["pe"]
            p.w("dv", "pe", trc)
            if tt == 0 and half == 0 and tspare is not None:
                p.w("ac", "pe", trc)
                acc_ = ac(lambda e: e.activation(tspare[:], ps_trA[:, :], AF.Copy))
                p.w("sy", "ac", acc_)
                store(tap_t["pstr0"].ap(), tspare[:])
                p.w("pe", "ac", acc_)
            dv(lambda e, half=half: e.tensor_copy(
                stg32[:, half * QW:(half + 1) * QW], ps_trh[half][:, :]))
            evc = p.cnt["dv"]
            p.w("pe", "dv", evc)
        p.w("sy", "dv", evc)
        if tt == 0 and "st32" in tap_t:
            store(tap_t["st32"].ap(), stg32[:])
        st_x1 = store(x1spill[128 * tt:128 * (tt + 1), :], stg32[:])
        st_x1 = p.cnt["st"]
    S3z.close()

    S4 = ExitStack()
    sb4 = lambda n, s, d: S4.enter_context(nc.sbuf_tensor(n, s, d))
    rwT = sb4("rwT", [128, 64], BF16)
    rbt = sb4("rbt", [128, 8], F32)
    ltri = sb4("ltri", [128, 128], F32)
    ltok = sb4("ltok", [128, 16 * 8], F32)
    gstrip = sb4("gstrip", [128, 16 * 8], F32)
    mstrip = sb4("mstrip", [128, 16 * 8], F32)
    grtok = sb4("grtok", [128, 16 * 8], F32)
    totrow = sb4("totrow", [1, 8], F32)
    prevrow = sb4("prevrow", [1, 8], F32)
    crow = sb4("crow", [1, 64], F32)
    cmask = sb4("cmask", [1, 64], F32)
    offrow = sb4("offrow", [1, 8], F32)
    agb = sb4("agb", [128, 8], F32)
    cin = sb4("cin", [8, 8], F32)
    ccol = sb4("ccol", [8, 1], F32)
    t41 = sb4("t41", [128, 8], F32)
    t42 = sb4("t42", [128, 8], F32)
    t43 = sb4("t43", [128, 8], F32)
    m1c = sb4("m1c", [128, 1], F32)
    m2c = sb4("m2c", [128, 1], F32)
    oc1 = sb4("oc1", [128, 1], F32)
    oc2 = sb4("oc2", [128, 1], F32)
    g1c = sb4("g1c", [128, 1], F32)
    g2c = sb4("g2c", [128, 1], F32)
    stg1 = sb4("stg1", [128, D], BF16)
    stg2 = sb4("stg2", [128, D], BF16)
    x4t = sb4("x4t", [128, D], F32)
    oint1 = sb4("oint1", [128, 1], I32)
    oint2 = sb4("oint2", [128, 1], I32)
    ps_l8 = S4.enter_context(nc.psum_tensor("ps_l8", [128, 8], F32))
    ps_rk = S4.enter_context(nc.psum_tensor("ps_rk", [128, 8], F32))
    ps_tot = S4.enter_context(nc.psum_tensor("ps_tot", [1, 8], F32))
    ps_ab = S4.enter_context(nc.psum_tensor("ps_ab", [128, 8], F32))

    for _eng in ("sy", "gp"):
        p.w(_eng, "pe")
        p.w(_eng, "ac")
        p.w(_eng, "dv")
        p.w(_eng, "st")
    if "x1sp4" in tap_t:
        store(tap_t["x1sp4"].ap(), x1spill[0:128, :], eng="gp")
    ld_rwt = load(rwT[:], rwt_d[:, :], eng="gp")
    ld_rb = load(rbt[:], rbt_d[:, :])
    ld_lt = load(ltri[:], ltri_d[:, :])

    p.w("pe", "gd", ld_rwt)
    p.w("pe", "dv", x1b_done)
    evl = None
    for tt in range(16):
        if evl is not None:
            p.w("pe", "dv", evl)
        for kc in range(8):
            f = (lambda e, tt=tt, kc=kc: e.matmul(
                ps_l8[:, :], x1b[:, kc * T + 128 * tt: kc * T + 128 * (tt + 1)],
                rwT[:, 8 * kc:8 * (kc + 1)], start=(kc == 0), stop=(kc == 7)))
            pei(f) if kc == 7 else pe(f)
        lmc = p.cnt["pe"]
        p.w("dv", "pe", lmc)
        p.w("dv", "ld", ld_rb)
        evl = dv(lambda e, tt=tt: e.tensor_add(
            ltok[:, 8 * tt:8 * (tt + 1)], ps_l8[:, :], rbt[:]))
    ltok_done = p.cnt["dv"]
    if "ltok" in tap_t:
        p.w("sy", "dv", ltok_done)
        store(tap_t["ltok"].ap(), ltok[:])

    for tt in range(16):
        lt = ltok[:, 8 * tt:8 * (tt + 1)]
        dv(lambda e, lt=lt: e.tensor_reduce(m1c[:], lt, AX, OP.max))
        dv(lambda e: e.tensor_scalar_mul(m2c[:], m1c[:], -1.0))
        md = p.cnt["dv"]
        p.w("ac", "dv", md)
        ec = ac(lambda e, lt=lt: e.activation(t41[:], lt, AF.Exp,
                                              bias=m2c[:, 0:1], scale=1.0))
        p.w("dv", "ac", ec)
        dv(lambda e: e.tensor_reduce(g1c[:], t41[:], AX, OP.add))
        dv(lambda e: e.reciprocal(g1c[:], g1c[:]))
        dv(lambda e: e.tensor_scalar(t41[:], t41[:], g1c[:, 0:1], None, OP.mult))
        dv(lambda e, lt=lt: e.tensor_scalar(t42[:], lt, m1c[:, 0:1], None,
                                            OP.is_ge))
        dv(lambda e, lt=lt: e.scalar_tensor_tensor(t43[:], t42[:], -1.0e30, lt,
                                                   OP.mult, OP.add))
        dv(lambda e: e.tensor_reduce(m2c[:], t43[:], AX, OP.max))
        dv(lambda e, lt=lt: e.tensor_scalar(t42[:], lt, m2c[:, 0:1], None,
                                            OP.is_ge))
        dv(lambda e, tt=tt: e.tensor_mul(gstrip[:, 8 * tt:8 * (tt + 1)],
                                         t41[:], t42[:]))
        dv(lambda e, tt=tt: e.tensor_copy(mstrip[:, 8 * tt:8 * (tt + 1)],
                                          t42[:]))
        dv(lambda e, tt=tt: e.tensor_scalar(npsc[:, tt:tt + 1],
                                            gstrip[:, 8 * tt + 7:8 * tt + 8],
                                            1.0, None, OP.add))
    top2_done = p.cnt["dv"]
    if "gates" in tap_t:
        p.w("sy", "dv", top2_done)
        store(tap_t["gates"].ap(), gstrip[:])

    dv(lambda e: e.memset(prevrow[:], 0.0))
    pr_done = p.cnt["dv"]
    p.w("pe", "dv", top2_done)
    p.w("pe", "ld", ld_lt)
    for tt in range(16):
        p.w("pe", "dv", pr_done)
        pe(lambda e, tt=tt: e.matmul(
            ps_rk[:, :], ltri[:], mstrip[:, 8 * tt:8 * (tt + 1)],
            start=True, stop=False, skip_group_check=True))
        pei(lambda e, tt=tt: e.matmul(
            ps_rk[:, :], onesrow[:], prevrow[:],
            start=False, stop=True, skip_group_check=True))
        rkc = p.cnt["pe"]
        pei(lambda e, tt=tt: e.matmul(
            ps_tot[:, :], onescol128f[:], mstrip[:, 8 * tt:8 * (tt + 1)],
            start=(tt == 0), stop=(tt == 15)))
        totc = p.cnt["pe"]
        p.w("dv", "pe", rkc)
        dv(lambda e, tt=tt: e.tensor_copy(grtok[:, 8 * tt:8 * (tt + 1)],
                                          ps_rk[:, :]))
        p.w("dv", "pe", totc)
        dv(lambda e: e.tensor_copy(totrow[:], ps_tot[:, :]))
        pr_done = dv(lambda e: e.tensor_add(prevrow[:], totrow[:], totrow[:]))
        pr_done = dv(lambda e: e.tensor_copy(prevrow[:], totrow[:]))
    lrank_done = p.cnt["dv"]

    p.w("sy", "dv", lrank_done)
    st_ct = store(scr_d.ap()[40:41, 0:8], totrow[:])
    p.w("sy", "st", st_ct)
    ld_cc = load(ccol[:], _ap(scr_d.ap(), 40 * 8, [[1, 8], [8, 1]]), slow=True)
    dv(lambda e: e.memset(cin[:], 0.0))
    p.w("dv", "ld", ld_cc)
    cv = dv(lambda e: e.tensor_copy(cin[:, 0:1], ccol[:]))
    p.w("gp", "dv", cv)
    store(cntag_in.ap(), cin[:], eng="gp")
    p.w("gp", "gd")
    cc_cnt = gpc(lambda e: e.collective_compute(
        "AllGather", OP.bypass, replica_groups=[[0, 1, 2, 3, 4, 5, 6, 7]],
        ins=[cntag_in.ap()], outs=[cntag_out.ap()]))
    p.w("sy", "gp", cc_cnt)
    ld_cr = load(crow[:], _ap(cntag_out.ap(), 0, [[512, 1], [8, 64]]), slow=True)
    p.w("dv", "ld", ld_cr)
    dv(lambda e: e.tensor_mul(cmask[:], crow[:], agm[:]))
    for e_ in range(7):
        src_ = _ap(cmask, e_, [[64, 1], [8, 8]])
        dv(lambda e, s=src_, e_=e_: e.tensor_reduce(offrow[:, e_:e_ + 1], s,
                                                    AX, OP.add))
    dv(lambda e: e.memset(offrow[:, 7:8], 0.0))
    ofr = p.cnt["dv"]
    p.w("pe", "dv", ofr)
    pei(lambda e: e.matmul(ps_ab[:, :], onesrow[:], offrow[:],
                           start=True, stop=True))
    abc = p.cnt["pe"]
    p.w("dv", "pe", abc)
    dv(lambda e: e.tensor_copy(agb[:], ps_ab[:, :]))
    for tt in range(16):
        dv(lambda e, tt=tt: e.tensor_add(grtok[:, 8 * tt:8 * (tt + 1)],
                                         grtok[:, 8 * tt:8 * (tt + 1)], agb[:]))
    grt_done = p.cnt["dv"]
    if "grtok" in tap_t:
        p.w("sy", "dv", grt_done)
        store(tap_t["grtok"].ap(), grtok[:])

    sc_hist = None
    for tt in range(16):
        gr = grtok[:, 8 * tt:8 * tt + 7]
        mk = mstrip[:, 8 * tt:8 * tt + 7]
        gt = gstrip[:, 8 * tt:8 * tt + 7]
        dv(lambda e, gr=gr: e.tensor_scalar(t41[:, 0:7], gr, float(CAP), None,
                                            OP.is_lt))
        dv(lambda e, mk=mk: e.tensor_mul(t41[:, 0:7], t41[:, 0:7], mk))
        dv(lambda e, gr=gr: e.tensor_add(t42[:, 0:7], gr, eoff[:, 0:7]))
        dv(lambda e: e.tensor_scalar(t42[:, 0:7], t42[:, 0:7], float(DUMP_X),
                                     None, OP.subtract))
        dv(lambda e: e.tensor_mul(t42[:, 0:7], t42[:, 0:7], t41[:, 0:7]))
        dv(lambda e: e.tensor_scalar(t42[:, 0:7], t42[:, 0:7], float(DUMP_X),
                                     None, OP.add))
        dv(lambda e: e.tensor_reduce(oc1[:], t42[:, 0:7], AX, OP.min))
        dv(lambda e: e.tensor_scalar(t43[:, 0:7], t42[:, 0:7], oc1[:, 0:1],
                                     None, OP.is_equal))
        dv(lambda e, gt=gt: e.tensor_mul(t41[:, 0:7], t43[:, 0:7], gt))
        dv(lambda e: e.tensor_reduce(g1c[:], t41[:, 0:7], AX, OP.add))
        dv(lambda e: e.scalar_tensor_tensor(t43[:, 0:7], t43[:, 0:7], 30000.0,
                                            t42[:, 0:7], OP.mult, OP.add))
        dv(lambda e: e.tensor_reduce(oc2[:], t43[:, 0:7], AX, OP.min))
        dv(lambda e: e.tensor_scalar_min(oc2[:], oc2[:], float(DUMP_X)))
        dv(lambda e: e.tensor_scalar(t41[:, 0:7], t42[:, 0:7], oc2[:, 0:1],
                                     None, OP.is_equal))
        dv(lambda e, gt=gt: e.tensor_mul(t41[:, 0:7], t41[:, 0:7], gt))
        dv(lambda e: e.tensor_reduce(g2c[:], t41[:, 0:7], AX, OP.add))
        dv(lambda e: e.tensor_copy(oint1[:], oc1[:]))
        dv(lambda e: e.tensor_copy(oint2[:], oc2[:]))
        dv(lambda e: e.tensor_scalar(t41[:, 0:1], oc1[:], float(DUMP_X), None,
                                     OP.is_ge))
        dv(lambda e: e.scalar_tensor_tensor(
            t42[:, 0:1], t41[:, 0:1], float(DUMP_G - DUMP_X), oc1[:],
            OP.mult, OP.add))
        dv(lambda e: e.tensor_scalar(t41[:, 1:2], oc2[:], float(DUMP_X), None,
                                     OP.is_ge))
        dv(lambda e: e.scalar_tensor_tensor(
            t42[:, 1:2], t41[:, 1:2], float(DUMP_G - DUMP_X), oc2[:],
            OP.mult, OP.add))
        dv(lambda e, tt=tt: e.tensor_copy(offi[:, 2 * tt:2 * tt + 2],
                                          t42[:, 0:2]))
        if sc_hist is not None:
            p.w("dv", "gd", sc_hist)
        p.w("sy", "dv")
        lx4 = load(x4t[:], x1spill[128 * tt:128 * (tt + 1), :])
        p.w("dv", "ld", lx4)
        if tt == 0 and "x4t" in tap_t:
            p.w("sy", "ld", lx4)
            store(tap_t["x4t"].ap(), x4t[:])
            p.w("dv", "st")
        dv(lambda e, tt=tt: e.tensor_scalar(stg1[:], x4t[:],
                                            g1c[:, 0:1], None, OP.mult))
        dv(lambda e, tt=tt: e.tensor_scalar(stg2[:], x4t[:],
                                            g2c[:, 0:1], None, OP.mult))
        stgd = p.cnt["dv"]
        if tt == 0 and "stg1" in tap_t:
            p.w("gp", "dv", stgd)
            store(tap_t["stg1"].ap(), stg1[:], eng="gp")
            gtmp = dv(lambda e: e.tensor_copy(t43[:, 0:1], g1c[:]))
            gtmp = dv(lambda e: e.tensor_copy(t43[:, 1:2], g2c[:]))
            p.w("gp", "dv", gtmp)
            store(tap_t["g1"].ap(), t43[:, 0:2], eng="gp")
            p.w("dv", "gd")
        p.w("gp", "dv", stgd)
        p.w("gp", "gd", gd_zero)
        p.op("gp", lambda e: e.indirect_dma_start(
            out=xtable.ap(), out_offset=IndirectOffsetOnAxis(oint1[:], 0),
            in_=stg1[:], in_offset=None), inc="gd", n=16)
        sc_hist = p.op("gp", lambda e: e.indirect_dma_start(
            out=xtable.ap(), out_offset=IndirectOffsetOnAxis(oint2[:], 0),
            in_=stg2[:], in_offset=None), inc="gd", n=16)
    scat_done = p.cnt["gd"]
    offi_done = p.cnt["dv"]
    if "offs" in tap_t:
        offs_f = S4.enter_context(nc.sbuf_tensor("offs_f", [128, 32], F32))
        p.w("dv", "dv", offi_done)
        cof = dv(lambda e: e.tensor_copy(offs_f[:], offi[:]))
        p.w("sy", "dv", cof)
        store(tap_t["offs"].ap(), offs_f[:])

    p.w("gp", "gd", scat_done)
    if "xtb" in tap_t:
        store(tap_t["xtb"].ap(), xtable[2048:2304, :], eng="gp")
    cc_rs = gpc(lambda e: e.collective_compute(
        "ReduceScatter", OP.add, replica_groups=[[0, 1, 2, 3, 4, 5, 6, 7]],
        ins=[xtable.ap()], outs=[rsout_d.ap()]))
    S4.close()
    S3.close()
    S_o.close()

    S5 = ExitStack()
    sb5 = lambda n, s, d: S5.enter_context(nc.sbuf_tensor(n, s, d))
    wibh0 = sb5("wibh0", [128, 8 * 2048], BF16)
    wibh1 = sb5("wibh1", [128, 8 * 2048], BF16)
    wibhs = [wibh0, wibh1]
    hstr = sb5("hstr", [128, 32 * 1024], BF16)
    xgT = sb5("xgT", [128, 8 * 1024], BF16)
    rstl = sb5("rstl", [128, 2 * 1024], BF16)
    wos = sb5("wos", [128, 32 * 256], BF16)
    ofm = sb5("ofm", [128, 8 * 1024], BF16)
    stg5 = sb5("stg5", [128, D], BF16)
    ps_i = S5.enter_context(nc.psum_tensor("ps_i", [128, 2 * QW], F32))
    ps_o = S5.enter_context(nc.psum_tensor("ps_o", [128, 4 * QW], F32))
    ps_xb = S5.enter_context(nc.psum_tensor("ps_xb", [128, 1024], BF16))

    for halfw in range(2):
        for kc in range(8):
            load(wibhs[halfw][:, kc * 2048:(kc + 1) * 2048],
                 wi_d[128 * kc:128 * (kc + 1),
                      2048 * halfw:2048 * (halfw + 1)], eng="gp")
    gd_wi_all = p.cnt["gd"]
    wo_pre = load(wos[:].rearrange("a (b c) -> a b c", b=32),
                  _ap(wo_d.ap(), 0, [[1024, 128], [131072, 32], [1, 256]]),
                  eng="gp")
    if "x1sp" in tap_t:
        p.w("gp", "st")
        store(tap_t["x1sp"].ap(), x1spill[0:256, :], eng="gp")
    if "rsx" in tap_t:
        p.w("gp", "gp")
        store(tap_t["rsx"].ap(), rsout_d[0:256, :], eng="gp")
    if "rsx2" in tap_t:
        p.w("gp", "gp")
        store(tap_t["rsx2"].ap(), rsout_d[1024 + 768:1024 + 1024, :], eng="gp")
    for quad in range(2):
        p.w("sy", "gp", cc_rs)
        for i in range(8):
            half = i % 2
            p.w("sy", "pe")
            ld5 = load(rstl[:, half * 1024:(half + 1) * 1024],
                       rsout_d[1024 * quad + 128 * i:
                               1024 * quad + 128 * (i + 1), :])
            p.w("pe", "ld", ld5)
            p.w("pe", "dv")
            for c in range(8):
                f = (lambda e, c=c, half=half: e.transpose(
                    ps_xb[:, 128 * c:128 * (c + 1)],
                    rstl[:, half * 1024 + 128 * c: half * 1024 + 128 * (c + 1)],
                    identb[:]))
                pei(f) if c == 7 else pe(f)
            trc = p.cnt["pe"]
            p.w("dv", "pe", trc)
            src = _ap(ps_xb, 0, [[1024, 128], [128, 8], [1, 128]])
            dst = _ap(xgT, 128 * i, [[8 * 1024, 128], [1024, 8], [1, 128]])
            dv(lambda e, s=src, d=dst: e.tensor_copy(d, s))
        xgt_done = p.cnt["dv"]

        if quad == 0 and "wib" in tap_t:
            p.w("gp", "gd", gd_wi_all)
            store(tap_t["wib"].ap(), wibh0[:, 0:2048], eng="gp")
        if quad == 0 and "wib2" in tap_t:
            p.w("gp", "gd", gd_wi_all)
            store(tap_t["wib2"].ap(), wibh1[:, 7 * 2048:8 * 2048], eng="gp")
        if quad == 0 and "xgt" in tap_t:
            p.w("gp", "dv", xgt_done)
            store(tap_t["xgt"].ap(), xgT[:, 0:1024], eng="gp")
        rel_hist = {}
        for halfw in range(2):
            wibh = wibhs[halfw]
            p.w("pe", "gd", gd_wi_all)
            p.w("pe", "dv", xgt_done)
            for oc_ in range(16):
                oc = 16 * halfw + oc_
                for st in range(2):
                    bank = (2 * oc_ + st) % 2
                    pst = ps_i[:, bank * QW:(bank + 1) * QW]
                    prev = rel_hist.get(bank)
                    if prev is not None:
                        p.w("pe", "ac", prev)
                    for kc in range(8):
                        f = (lambda e, oc_=oc_, st=st, kc=kc, pst=pst,
                             wibh=wibh: e.matmul(
                            pst,
                            wibh[:, kc * 2048 + 128 * oc_: kc * 2048 + 128 * (oc_ + 1)],
                            xgT[:, kc * 1024 + st * QW: kc * 1024 + (st + 1) * QW],
                            start=(kc == 0), stop=(kc == 7)))
                        pei(f) if kc == 7 else pe(f)
                    mm5 = p.cnt["pe"]
                    p.w("ac", "pe", mm5)
                    rl = ac(lambda e, oc=oc, st=st, pst=pst: e.activation(
                        hstr[:, oc * 1024 + st * QW: oc * 1024 + (st + 1) * QW],
                        pst, AF.Relu))
                    rel_hist[bank] = rl
        relu_done = p.cnt["ac"]
        if "hq" in tap_t and quad == 0:
            p.w("gp", "ac", relu_done)
            store(tap_t["hq"].ap(), hstr[:, 0:8 * 1024], eng="gp")

        p.w("pe", "ac", relu_done)
        for ofch in range(4):
            if quad == 0 and ofch == 0:
                ldw = wo_pre
            else:
                p.w("gp", "pe")
                src_ap = _ap(wo_d.ap(), 256 * ofch,
                             [[1024, 128], [131072, 32], [1, 256]])
                ldw = load(wos[:].rearrange("a (b c) -> a b c", b=32), src_ap,
                           eng="gp")
            p.w("pe", "gd", ldw)
            for ofq in range(2):
                for st in range(2):
                    boff = (2 * ofq + st) * QW
                    for ic in range(32):
                        f = (lambda e, ic=ic, ofq=ofq, st=st, boff=boff:
                             e.matmul(
                                 _ap(ps_o, boff, [[4 * QW, 128], [1, QW]]),
                                 wos[:, ic * 256 + 128 * ofq:
                                     ic * 256 + 128 * (ofq + 1)],
                                 hstr[:, ic * 1024 + st * QW:
                                      ic * 1024 + (st + 1) * QW],
                                 start=(ic == 0), stop=(ic == 31)))
                        pei(f) if ic == 31 else pe(f)
            omm = p.cnt["pe"]
            p.w("dv", "pe", omm)
            for ofq in range(2):
                for st in range(2):
                    boff = (2 * ofq + st) * QW
                    ofc = 2 * ofch + ofq
                    dv(lambda e, boff=boff, ofc=ofc, st=st: e.tensor_copy(
                        ofm[:, ofc * 1024 + st * QW: ofc * 1024 + (st + 1) * QW],
                        _ap(ps_o, boff, [[4 * QW, 128], [1, QW]])))
            evo = p.cnt["dv"]
            p.w("pe", "dv", evo)
        ofm_done = p.cnt["dv"]

        st5 = None
        for i4 in range(8):
            if st5 is not None:
                p.w("dv", "st", st5)
            p.w("pe", "dv", ofm_done)
            p.w("pe", "dv")
            for c in range(8):
                f = (lambda e, i4=i4, c=c: e.transpose(
                    ps_xb[:, 128 * c:128 * (c + 1)],
                    ofm[:, c * 1024 + 128 * i4: c * 1024 + 128 * (i4 + 1)],
                    identb[:]))
                pei(f) if c == 7 else pe(f)
            trc5 = p.cnt["pe"]
            p.w("dv", "pe", trc5)
            dv(lambda e: e.tensor_copy(stg5[:], ps_xb[:, :]))
            evs = p.cnt["dv"]
            p.w("sy", "dv", evs)
            st5 = store(outag_in[1024 * quad + 128 * i4:
                                 1024 * quad + 128 * (i4 + 1), :], stg5[:])
    exp_done = p.cnt["st"]

    p.w("gp", "st", exp_done)
    if "eout" in tap_t:
        store(tap_t["eout"].ap(), outag_in[0:256, :], eng="gp")
        p.w("gp", "gd")
    cc_out = gpc(lambda e: e.collective_compute(
        "AllGather", OP.bypass, replica_groups=[[0, 1, 2, 3, 4, 5, 6, 7]],
        ins=[outag_in.ap()], outs=[outag_out.ap()[0:TBLR, :]]))
    S5.close()

    S6 = ExitStack()
    sb6 = lambda n, s, d: S6.enter_context(nc.sbuf_tensor(n, s, d))
    g1r = sb6("g1r", [128, D], BF16)
    g2r = sb6("g2r", [128, D], BF16)
    x1t = sb6("x1t", [128, D], F32)
    z2 = sb6("z2", [128, D], F32)
    sqs = sb6("sqs", [128, D], F32)
    s1c = sb6("s1c", [128, 1], F32)
    s2c = sb6("s2c", [128, 1], F32)
    muc = sb6("muc", [128, 1], F32)
    vrc = sb6("vrc", [128, 1], F32)
    rsc = sb6("rsc", [128, 1], F32)
    ytile = sb6("ytile", [128, D], F32)

    for _eng in ("sy", "gp"):
        p.w(_eng, "pe")
        p.w(_eng, "ac")
        p.w(_eng, "dv")
    st_y = None
    prev_z = None
    for tt in range(16):
        p.w("gp", "gp")
        p.w("gp", "dv", offi_done)
        if prev_z is not None:
            p.w("gp", "dv", prev_z)
        gl1 = p.op("gp", lambda e, tt=tt: e.indirect_dma_start(
            out=g1r[:], out_offset=None, in_=outag_out.ap(),
            in_offset=IndirectOffsetOnAxis(offi[:, 2 * tt:2 * tt + 1], 0)),
            inc="gd", n=16)
        gl2 = p.op("gp", lambda e, tt=tt: e.indirect_dma_start(
            out=g2r[:], out_offset=None, in_=outag_out.ap(),
            in_offset=IndirectOffsetOnAxis(offi[:, 2 * tt + 1:2 * tt + 2], 0)),
            inc="gd", n=16)
        if st_y is not None:
            p.w("sy", "dv")
        lx = load(x1t[:], x1spill[128 * tt:128 * (tt + 1), :])
        p.w("dv", "gd", gl2)
        p.w("dv", "ld", lx)
        if st_y is not None:
            p.w("dv", "st", st_y)
        dv(lambda e, tt=tt: e.tensor_scalar(z2[:], x1t[:], npsc[:, tt:tt + 1],
                                            None, OP.mult))
        dv(lambda e: e.tensor_add(z2[:], z2[:], g1r[:]))
        dv(lambda e: e.tensor_add(z2[:], z2[:], g2r[:]))
        prev_z = p.cnt["dv"]
        dv(lambda e: e.tensor_reduce(s1c[:], z2[:], AX, OP.add))
        zc = p.cnt["dv"]
        p.w("ac", "dv", zc)
        sq6 = ac(lambda e: e.activation(sqs[:], z2[:], AF.Square,
                                        accum_out=s2c[:, 0:1]))
        p.w("dv", "ac", sq6)
        dv(lambda e: e.tensor_scalar_mul(muc[:], s1c[:], 1.0 / D))
        dv(lambda e: e.tensor_mul(vrc[:], muc[:], muc[:]))
        dv(lambda e: e.scalar_tensor_tensor(vrc[:], s2c[:], 1.0 / D, vrc[:],
                                            OP.mult, OP.subtract))
        vc6 = p.cnt["dv"]
        p.w("ac", "dv", vc6)
        sd6 = ac(lambda e: e.activation(rsc[:], vrc[:], AF.Sqrt, bias=eps128[:, 0:1]))
        p.w("dv", "ac", sd6)
        dv(lambda e: e.reciprocal(rsc[:], rsc[:]))
        dv(lambda e: e.tensor_scalar(ytile[:], z2[:], muc[:, 0:1], rsc[:, 0:1],
                                     OP.subtract, OP.mult))
        yc = p.cnt["dv"]
        p.w("sy", "dv", yc)
        st_y = store(y_d[128 * tt:128 * (tt + 1), :], ytile[:])

    p.w("sy", "st", p.cnt["st"])
    p.w("sy", "gd", p.cnt["gd"])
    p.w("sy", "ld", p.cnt["ld"])
    S6.close()
    S0.close()

    p.emit()
    p.close()
    return nc, tap_t



def _prep_inputs(inputs):
    x = np.asarray(inputs["L0_attn_x"], np.float32).reshape(NTOK, D)
    wqkv = np.ascontiguousarray(np.asarray(inputs["L0_attn_Wqkv"], np.float32))
    bqkv = np.ascontiguousarray(
        np.asarray(inputs["L0_attn_bqkv"], np.float32).reshape(3 * D, 1))
    wout = np.ascontiguousarray(np.asarray(inputs["L0_attn_Wout"], np.float32))
    bout = np.ascontiguousarray(
        np.asarray(inputs["L0_attn_bout"], np.float32).reshape(D, 1))
    rw = np.asarray(inputs["L0_ffn_router_w"], np.float32)
    rb = np.asarray(inputs["L0_ffn_router_b"], np.float32)
    rwT = np.zeros((128, 64), np.float32)
    for kc in range(8):
        rwT[:, 8 * kc:8 * (kc + 1)] = rw[:, 128 * kc:128 * (kc + 1)].T
    rbtile = np.tile(rb.reshape(1, NE), (128, 1)).astype(np.float32)
    jt = np.arange(128)
    lt_tri = (jt[:, None] < jt[None, :]).astype(np.float32)
    wi = np.asarray(inputs["L0_ffn_experts_inter"], np.float32)
    wo = np.asarray(inputs["L0_ffn_experts_out"], np.float32)
    strip, ident, eoff, onesrow, onescol, sel2 = _host_shared()
    in_maps = []
    for c in range(NCORES):
        beta, cints, agmask, kinds, augq = _host_tables(c)
        xs = np.ascontiguousarray(x[T * c:T * (c + 1)].T)
        ex = c if c < NRE else 0
        in_maps.append({
            "xT": xs, "wqkv": wqkv, "bqkv": bqkv,
            "wout": wout, "bout": bout, "rwTh": rwT, "rbtile": rbtile, "lt_tri": lt_tri,
            "my_wi": np.ascontiguousarray(wi[ex]),
            "my_wo": np.ascontiguousarray(wo[ex]),
            "betaslab": beta, "coreints": cints,
            "agmask": agmask, "stripm": strip, "kinds": kinds, "augq": augq,
            "identbf": ident, "eofftile": eoff, "onesrow": onesrow,
            "onescol": onescol, "sel2": sel2,
        })
    return in_maps


def run(taps=(), inputs=None, **kw):
    nc, tap_t = build_program(taps)
    in_maps = _prep_inputs(inputs)
    return run_bass_kernel_spmd(nc, in_maps, core_ids=list(range(NCORES)), **kw)


def kernel(**inputs):
    res = run(taps=(), inputs=inputs)
    y = np.concatenate([r["y"] for r in res.results], axis=0)
    return y.reshape(2, SEQ, D)


# version 17
# speedup vs baseline: 1.0286x; 1.0286x over previous
import math
from contextlib import ExitStack

import numpy as np
import ml_dtypes

import concourse.bass as bass
import concourse.mybir as mybir
from concourse.bass_core_rust import IndirectOffsetOnAxis
from concourse.bass_utils import run_bass_kernel_spmd

F32 = mybir.dt.float32
BF16 = mybir.dt.bfloat16
I32 = mybir.dt.int32
AF = mybir.ActivationFunctionType
OP = mybir.AluOpType
AX = mybir.AxisListType.X

NCORES = 8
T = 2048
SEQ = 8192
D = 1024
NH = 16
DH = 64
NQT = 4
QW = 512
NKB = 64
NE = 8
NRE = 7
CAP = 2048
DI = 4096
TBLR = 16384
DUMP_X = 14336
DUMP_G = 16384
BETA_KILL = -60000.0
NTOK = NCORES * T

ENGS = ("sy", "gp", "pe", "dv", "ac")


def _slopes16():
    start = 2.0 ** (-(2.0 ** -(math.log2(NH) - 3)))
    return [start ** i for i in range(NH)]


SLOPES = _slopes16()


def _host_tables(core):
    my_base = (core % 4) * T
    beta = np.zeros((NH, 128, 257), np.float32)
    p = np.arange(128, dtype=np.float64)
    for h in range(NH):
        s = SLOPES[h]
        for qt in range(NQT):
            i0 = my_base + QW * qt
            for bk in range(NKB):
                j0 = 128 * bk
                col = qt * NKB + bk
                if i0 <= j0 < i0 + QW:
                    beta[h, :, col] = BETA_KILL
                elif j0 + 127 < i0:
                    beta[h, :, col] = (-s * (i0 - j0 - p) - 256.0 * s).astype(np.float32)
                else:
                    beta[h, :, col] = (-s * (j0 + p - i0) + 256.0 * s).astype(np.float32)
        beta[h, :, 256] = 0.0
    coreints = np.zeros((1, 8), np.int32)
    coreints[0, 0] = my_base
    agmask = np.zeros((1, 64), np.float32)
    agmask[0, :8 * core] = 1.0
    kinds = np.zeros((5, SEQ), np.float64)
    kinds[0] = 1.0
    jj = np.arange(SEQ)
    for qt in range(NQT):
        kinds[1 + qt] = (jj < my_base + QW * qt).astype(np.float64)
    augq = np.zeros((5, NH * T), np.float64)
    a = np.arange(QW, dtype=np.float64) - 256.0
    for h in range(NH):
        s = SLOPES[h]
        for qt in range(NQT):
            colb = h * T + qt * QW
            augq[0, colb:colb + QW] = 8.0 * s * a
            augq[1 + qt, colb:colb + QW] = -16.0 * s * a
    return (beta, coreints, agmask,
            kinds.astype(ml_dtypes.bfloat16), augq.astype(ml_dtypes.bfloat16))


def _host_shared():
    p = np.arange(128, dtype=np.float64)[:, None]
    m = np.arange(896, dtype=np.float64)[None, :]
    strip = (-8.0 * np.abs(p - m + 384.0)).astype(np.float32)
    ident = np.eye(128).astype(ml_dtypes.bfloat16)
    eoff = np.zeros((128, 8), np.float32)
    eoff[:, :NRE] = (np.arange(NRE) * CAP)[None, :].astype(np.float32)
    onesrow = np.ones((1, 128), np.float32)
    onescol = np.ones((128, 1)).astype(ml_dtypes.bfloat16)
    sel2 = np.zeros((128, 2), np.float32)
    sel2[0:64, 0] = 1.0
    sel2[64:128, 1] = 1.0
    return strip, ident, eoff, onesrow, onescol, sel2.astype(ml_dtypes.bfloat16)


class _Prog:
    def __init__(self, nc):
        self.nc = nc
        self.ops = {e: [] for e in ENGS}
        self.cnt = {}
        self.waited = {e: {} for e in ENGS}
        self._sems = {}
        self._stack = ExitStack()

    def sem(self, name):
        s = self._stack.enter_context(self.nc.semaphore(name))
        self._sems[name] = s
        self.cnt[name] = 0

    def w(self, e, sem, val=None):
        if val is None:
            val = self.cnt[sem]
        if val <= 0 or self.waited[e].get(sem, 0) >= val:
            return
        self.waited[e][sem] = val
        s = self._sems[sem]
        self.ops[e].append(lambda eng, s=s, v=val: eng.wait_ge(s, v))

    def op(self, e, fn, inc=None, n=1):
        if inc is None:
            self.ops[e].append(fn)
            return None
        s = self._sems[inc]
        self.ops[e].append(lambda eng, fn=fn, s=s, n=n: fn(eng).then_inc(s, n))
        self.cnt[inc] += n
        return self.cnt[inc]

    def raw(self, e, fn):
        self.ops[e].append(fn)

    def emit(self):
        with self.nc.Block() as block:
            @block.sync
            def _(eng):
                for f in self.ops["sy"]:
                    f(eng)

            @block.gpsimd
            def _(eng):
                for f in self.ops["gp"]:
                    f(eng)

            @block.tensor
            def _(eng):
                for f in self.ops["pe"]:
                    f(eng)

            @block.vector
            def _(eng):
                for f in self.ops["dv"]:
                    f(eng)

            @block.scalar
            def _(eng):
                for f in self.ops["ac"]:
                    f(eng)

    def close(self):
        self._stack.close()


def _ap(t_or_ap, off, pattern):
    tensor = t_or_ap.tensor if isinstance(t_or_ap, bass.AP) else t_or_ap
    return bass.AP(tensor, off, pattern)


def build_program(taps=()):
    nc = bass.Bass("TRN2", target_bir_lowering=False, debug=False)
    taps = tuple(taps)

    def din(name, shape, dtype=F32):
        return nc.dram_tensor(name, shape, dtype, kind="ExternalInput")

    xT_d = din("xT", [D, T])
    wqkv_d = din("wqkv", [D, 3 * D])
    bqkv_d = din("bqkv", [3 * D, 1])
    wout_d = din("wout", [D, D])
    bout_d = din("bout", [D, 1])
    rwt_d = din("rwTh", [128, 64])
    rbt_d = din("rbtile", [128, 8])
    ltri_d = din("lt_tri", [128, 128])
    wi_d = din("my_wi", [D, DI])
    wo_d = din("my_wo", [DI, D])
    slab_d = din("betaslab", [NH, 128, 257])
    cints_d = din("coreints", [1, 8], I32)
    augq_d = din("augq", [5, NH * T], BF16)
    kinds_d = din("kinds", [5, SEQ], BF16)
    agm_d = din("agmask", [1, 64])
    stripm_d = din("stripm", [128, 896])
    ident_d = din("identbf", [128, 128], BF16)
    eoff_d = din("eofftile", [128, 8])
    onesr_d = din("onesrow", [1, 128])
    onesc_d = din("onescol", [128, 1], BF16)
    sel2_d = din("sel2", [128, 2], BF16)

    y_d = nc.dram_tensor("y", [T, D], F32, kind="ExternalOutput")

    kvag_in = nc.dram_tensor("kvag_in", [2 * D, T], BF16)
    kvag_out = nc.dram_tensor("kvag_out", [4 * 2 * D, T], BF16)
    knag_in = nc.dram_tensor("knag_in", [NH, 8], F32)
    knag_out = nc.dram_tensor("knag_out", [4 * NH, 8], F32)
    cntag_in = nc.dram_tensor("cntag_in", [8, 8], F32)
    cntag_out = nc.dram_tensor("cntag_out", [64, 8], F32)
    xtable = nc.dram_tensor("xtable", [TBLR, D], BF16)
    rsout_d = nc.dram_tensor("rs_out", [CAP, D], BF16)
    outag_in = nc.dram_tensor("outag_in", [CAP, D], BF16)
    outag_out = nc.dram_tensor("outag_out", [TBLR + 1, D], BF16)
    x1spill = nc.dram_tensor("x1spill", [T, D], F32)
    scr_d = nc.dram_tensor("scratch", [64, 8], F32)
    scr2_d = nc.dram_tensor("scratch2", [32, 2], F32)

    tap_t = {}
    def mktap(name, shape, dtype):
        if name in taps:
            tap_t[name] = nc.dram_tensor("tap_" + name, shape, dtype,
                                         kind="ExternalOutput")
    mktap("qall", [64, 8 * T], F32)
    mktap("oall", [128, 8 * T], F32)
    mktap("zf", [128, 8 * T], F32)
    mktap("x1", [128, 8 * T], F32)
    mktap("ltok", [128, 16 * 8], F32)
    mktap("gates", [128, 16 * 8], F32)
    mktap("grtok", [128, 16 * 8], F32)
    mktap("offs", [128, 32], F32)
    mktap("brow", [1, NH], F32)
    mktap("devt0", [128, 2 * 257], F32)
    mktap("ev30", [65, QW], F32)
    mktap("tstr0", [128, QW], F32)
    mktap("vstr0", [128, 65 * NKB], F32)
    mktap("murows", [4, QW], F32)
    mktap("xtb", [256, D], F32)
    mktap("x1sp", [256, D], F32)
    mktap("stg1", [128, D], F32)
    mktap("g1", [128, 2], F32)
    mktap("x4t", [128, D], F32)
    mktap("x1sp4", [128, D], F32)
    mktap("st32", [128, D], F32)
    mktap("zft", [128, 128], F32)
    mktap("idf", [128, 128], F32)
    mktap("pstr0", [128, 512], F32)
    mktap("rsx", [256, D], F32)
    mktap("varrows", [4, QW], F32)
    mktap("vostr0", [128, 65 * 16], F32)
    mktap("pts0", [128, QW], F32)
    mktap("ptm0", [128, QW], F32)
    mktap("hq", [128, 8 * 1024], F32)
    mktap("eout", [256, D], F32)
    mktap("wib", [128, 2048], F32)
    mktap("xgt", [128, 1024], F32)
    mktap("wib2", [128, 2048], F32)
    mktap("rsx2", [256, D], F32)

    p = _Prog(nc)
    for s in ("ld", "st", "gd", "pe", "dv", "ac", "gp"):
        p.sem(s)

    def dv(fn):
        p.w("dv", "dv")
        return p.op("dv", fn, inc="dv")

    def ac(fn):
        p.w("ac", "ac")
        return p.op("ac", fn, inc="ac")

    def gpc(fn):
        p.w("gp", "gp")
        return p.op("gp", fn, inc="gp")

    def pe(fn):
        return p.op("pe", fn)

    def pei(fn):
        return p.op("pe", fn, inc="pe")

    def load(dst, src, eng="sy", slow=False):
        sem = "ld" if eng == "sy" else "gd"
        if slow:
            fn = lambda e, d=dst, s=src: e.dma_start(
                out=d, in_=s, allow_slow_non_contiguous=True)
        else:
            fn = lambda e, d=dst, s=src: e.dma_start(out=d, in_=s)
        return p.op(eng, fn, inc=sem, n=16)

    def store(dst, src, eng="sy", slow=False):
        sem = "st" if eng == "sy" else "gd"
        if slow:
            fn = lambda e, d=dst, s=src: e.dma_start(
                out=d, in_=s, allow_slow_non_contiguous=True)
        else:
            fn = lambda e, d=dst, s=src: e.dma_start(out=d, in_=s)
        return p.op(eng, fn, inc=sem, n=16)

    S0 = ExitStack()
    sb0 = lambda n, s, d: S0.enter_context(nc.sbuf_tensor(n, s, d))
    identb = sb0("identb", [128, 128], BF16)
    identf = sb0("identf", [128, 128], F32)
    stripm = sb0("stripm_s", [128, 896], F32)
    onesrow = sb0("onesrow_s", [1, 128], F32)
    onescol = sb0("onescol_s", [128, 1], BF16)
    sel2 = sb0("sel2_s", [128, 2], BF16)
    eoff = sb0("eoff_s", [128, 8], F32)
    cints = sb0("cints", [1, 8], I32)
    agm = sb0("agm", [1, 64], F32)
    b128 = sb0("b128", [128, NH], F32)
    brow = sb0("brow_s", [1, NH], F32)
    offi = sb0("offi", [128, 32], I32)
    npsc = sb0("npsc", [128, 16], F32)
    eps128 = sb0("eps128", [128, 1], F32)
    onescol128f = sb0("onescol128f", [128, 1], F32)
    zrowb = sb0("zrowb", [1, QW], BF16)
    zcolb = sb0("zcolb", [1, 65], BF16)
    zrows = sb0("zrows", [128, D], BF16)
    zrow32 = sb0("zrow32", [1, D], F32)

    for dst, src in ((identb[:], ident_d[:, :]),
                     (stripm[:], stripm_d[:, :]), (onesrow[:], onesr_d[:, :]),
                     (onescol[:], onesc_d[:, :]), (sel2[:], sel2_d[:, :]),
                     (eoff[:], eoff_d[:, :]),
                     (cints[:], cints_d[:, :]), (agm[:], agm_d[:, :])):
        load(dst, src)
    ld_const = p.cnt["ld"]
    p.w("dv", "ld", ld_const)
    dv(lambda e: e.tensor_copy(identf[:], identb[:]))
    dv(lambda e: e.memset(eps128[:], 1e-5))
    dv(lambda e: e.memset(onescol128f[:], 1.0))
    dv(lambda e: e.memset(zrowb[:], 0.0))
    dv(lambda e: e.memset(zcolb[:], 0.0))
    dv(lambda e: e.memset(zrows[:], 0.0))
    dv(lambda e: e.memset(zrow32[:], 0.0))
    zdone = p.cnt["dv"]
    p.w("gp", "dv", zdone)
    store(outag_out[DUMP_G:DUMP_G + 1, :], zrows[0:1, :], eng="gp")

    S_o = ExitStack()
    oall = S_o.enter_context(nc.sbuf_tensor("oall", [128, 8 * T], BF16))
    S1q = ExitStack()

    qaug = S1q.enter_context(nc.sbuf_tensor("qaug", [69, NH * T], BF16))
    S1 = ExitStack()
    sb1 = lambda n, s, d: S1.enter_context(nc.sbuf_tensor(n, s, d))
    xb = sb1("xb", [128, 8 * T], BF16)
    wb = sb1("wb", [128, 8 * 3 * D], BF16)
    bia = sb1("bia", [128, 24], F32)
    stg = sb1("stg", [128, 2 * T], BF16)
    sqt = sb1("sqt", [128, T], BF16)
    norm2k = sb1("norm2k", [2, 8], F32)
    qrow = sb1("qrow", [1, NH], F32)
    qn = sb1("qn", [NH, 1], F32)
    kn = sb1("kn", [NH, 1], F32)
    knago = sb1("knago", [NH, 4], F32)
    kin = sb1("kin", [NH, 8], F32)
    kmax2 = sb1("kmax2", [NH, 1], F32)
    prod = sb1("prod", [NH, 1], F32)
    bcol = sb1("bcol", [NH, 1], F32)
    ps_qkv = S1.enter_context(nc.psum_tensor("ps_qkv", [128, 2 * QW], F32))
    ps_n = S1.enter_context(nc.psum_tensor("ps_n", [2, T], F32))
    ps_b = S1.enter_context(nc.psum_tensor("ps_b", [128, NH], F32))

    for c in range(8):
        load(xb[:, c * T:(c + 1) * T], xT_d[128 * c:128 * (c + 1), :], eng="gp")
    for c in range(8):
        load(wb[:, c * 3 * D:(c + 1) * 3 * D],
             wqkv_d[128 * c:128 * (c + 1), :], eng="gp")
    gd_in = p.cnt["gd"]
    for j in range(24):
        load(bia[:, j:j + 1], bqkv_d[128 * j:128 * (j + 1), :])
    ld_b = p.cnt["ld"]
    ld_augq = load(qaug[64:69, :], augq_d[:, :])
    dv(lambda e: e.memset(norm2k[:], 0.0))
    nzero = p.cnt["dv"]

    p.w("pe", "gd", gd_in)
    p.w("ac", "ld", ld_b)

    ev_hist = {}
    st_hist = {}
    for t in range(24):
        is_q = t < 8
        for n in range(4):
            bank = (t * 4 + n) % 2
            pst = ps_qkv[:, bank * QW:(bank + 1) * QW]
            prev = ev_hist.get(bank)
            if prev is not None:
                p.w("pe", "ac", prev)
            for kc in range(8):
                f = (lambda e, pst=pst, kc=kc, t=t, n=n:
                     e.matmul(pst,
                              wb[:, kc * 3 * D + 128 * t: kc * 3 * D + 128 * (t + 1)],
                              xb[:, kc * T + n * QW: kc * T + (n + 1) * QW],
                              start=(kc == 0), stop=(kc == 7)))
                pei(f) if kc == 7 else pe(f)
            mmc = p.cnt["pe"]
            p.w("ac", "pe", mmc)
            if is_q:
                ac(lambda e, pst=pst, t=t, n=n: e.activation(
                    qaug[0:64, (2 * t) * T + n * QW: (2 * t) * T + (n + 1) * QW],
                    pst[0:64, :], AF.Identity, bias=bia[0:64, t:t + 1], scale=1.0))
                ev = ac(lambda e, pst=pst, t=t, n=n: e.activation(
                    qaug[0:64, (2 * t + 1) * T + n * QW:
                         (2 * t + 1) * T + (n + 1) * QW],
                    pst[64:128, :], AF.Identity, bias=bia[64:128, t:t + 1],
                    scale=1.0))
            else:
                half = t % 2
                prevst = st_hist.get(half)
                if prevst is not None:
                    p.w("ac", "st", prevst)
                ev = ac(lambda e, pst=pst, t=t, n=n, half=half:
                        e.activation(stg[:, half * T + n * QW: half * T + (n + 1) * QW],
                                     pst, AF.Identity, bias=bia[:, t:t + 1],
                                     scale=1.0))
            ev_hist[bank] = ev
        chunk_done = p.cnt["ac"]
        if not is_q:
            if t < 16:
                src_ = stg[:, (t % 2) * T:(t % 2 + 1) * T]
                p.w("dv", "dv", nzero)
                sv = ac(lambda e, s=src_: e.activation(sqt[:], s, AF.Square))
                p.w("pe", "ac", sv)
                for n in range(4):
                    f = (lambda e, n=n: e.matmul(
                        ps_n[:, n * QW:(n + 1) * QW], sel2[:],
                        sqt[:, n * QW:(n + 1) * QW], start=True, stop=True))
                    pei(f) if n == 3 else pe(f)
                p.w("dv", "pe")
                rmax = dv(lambda e, t=t: e.tensor_reduce(
                    norm2k[:, (t - 8):(t - 8) + 1], ps_n[:], AX, OP.max))
                p.w("ac", "dv", rmax)
                p.w("pe", "dv", rmax)
            rows = 128 * (t - 8)
            p.w("sy", "ac", chunk_done)
            stc = store(kvag_in[rows:rows + 128, :],
                        stg[:, (t % 2) * T:(t % 2 + 1) * T])
            st_hist[t % 2] = stc

    p.w("ac", "ac")
    for h in range(NH):
        sv = ac(lambda e, h=h: e.activation(sqt[0:64, :], qaug[0:64, h * T:(h + 1) * T],
                                            AF.Square))
        p.w("pe", "ac", sv)
        for n in range(4):
            f = (lambda e, n=n: e.matmul(
                ps_n[0:1, n * QW:(n + 1) * QW], onescol[0:64, :],
                sqt[0:64, n * QW:(n + 1) * QW], start=True, stop=True))
            pei(f) if n == 3 else pe(f)
        p.w("dv", "pe")
        rq = dv(lambda e, h=h: e.tensor_reduce(qrow[:, h:h + 1],
                                               ps_n[0:1, :], AX, OP.max))
        p.w("ac", "dv", rq)
        p.w("pe", "dv", rq)
    qn_done = p.cnt["dv"]

    if "qall" in tap_t:
        p.w("gp", "ac")
        store(tap_t["qall"].ap(), qaug[0:64, 0:8 * T], eng="gp")

    p.w("gp", "st")
    p.w("gp", "gd")
    cc_kv = gpc(lambda e: e.collective_compute(
        "AllGather", OP.bypass, replica_groups=[[0, 1, 2, 3], [4, 5, 6, 7]],
        ins=[kvag_in.ap()], outs=[kvag_out.ap()]))

    p.w("sy", "dv")
    st_n = store(scr2_d.ap()[0:8, 0:2].rearrange("a b -> b a"), norm2k[:],
                 slow=True)
    st_q = store(scr2_d.ap()[8:16, 0:2].rearrange("a b -> (a b)")
                 .rearrange("(a b) -> a b", a=1), qrow[:])
    scr2_flat = scr2_d.ap().rearrange("a b -> (a b)")
    p.w("sy", "st", st_q)
    ld1 = load(qn[:], scr2_flat[16:32].rearrange("(a b) -> a b", b=1))
    ld2 = load(kn[:], scr2_flat[0:16].rearrange("(a b) -> a b", b=1))
    dv(lambda e: e.memset(kin[:], 0.0))
    p.w("dv", "ld", ld2)
    dv(lambda e: e.tensor_copy(kin[:, 0:1], kn[:]))
    p.w("gp", "dv")
    store(knag_in.ap(), kin[:], eng="gp")
    p.w("gp", "gd")
    cc_kn = gpc(lambda e: e.collective_compute(
        "AllGather", OP.bypass, replica_groups=[[0, 1, 2, 3], [4, 5, 6, 7]],
        ins=[knag_in.ap()], outs=[knag_out.ap()]))
    p.w("sy", "gp", cc_kn)
    ld3 = load(knago[:], _ap(knag_out.ap(), 0, [[8, 16], [128, 4]]), slow=True)
    p.w("dv", "ld", ld3)
    dv(lambda e: e.tensor_reduce(kmax2[:], knago[:], AX, OP.max))
    p.w("dv", "ld", ld1)
    dv(lambda e: e.tensor_mul(prod[:], qn[:], kmax2[:]))
    pc = p.cnt["dv"]
    p.w("ac", "dv", pc)
    bc = ac(lambda e: e.activation(bcol[:], prod[:], AF.Sqrt,
                                   scale=(1.02 / 8.0) ** 2))
    p.w("sy", "ac", bc)
    st_b = store(scr_d.ap()[0:16, 0:1], bcol[:], slow=True)
    p.w("sy", "st", st_b)
    ld4 = load(brow[:], _ap(scr_d.ap(), 0, [[512, 1], [8, 16]]), slow=True)
    if "brow" in tap_t:
        p.w("sy", "ld", ld4)
        store(tap_t["brow"].ap(), brow[:])
    p.w("pe", "ld", ld4)
    pei(lambda e: e.matmul(ps_b[:], onesrow[:], brow[:], start=True, stop=True))
    p.w("dv", "pe")
    dv(lambda e: e.tensor_copy(b128[:], ps_b[:]))
    b_done = p.cnt["dv"]
    S1.close()

    S2 = ExitStack()
    sb2 = lambda n, s, d: S2.enter_context(nc.sbuf_tensor(n, s, d))
    kaug = [sb2("kaug0", [69, SEQ], BF16), sb2("kaug1", [69, SEQ], BF16)]
    kown = [sb2("kown0", [64, T], BF16), sb2("kown1", [64, T], BF16)]
    vfm = sb2("vfm", [128, SEQ], BF16)
    vstr = [sb2("vstr0", [128, 65 * NKB], BF16),
            sb2("vstr1", [128, 65 * NKB], BF16)]
    vostr = [sb2("vostr0", [128, 65 * 16], BF16),
             sb2("vostr1", [128, 65 * 16], BF16)]
    devt = sb2("devt", [128, 2 * 257], F32)
    slabb = sb2("slabb", [128, 2 * 257], F32)
    strips = sb2("strips", [128, 2 * 896], BF16)
    ptb = sb2("ptb", [128, 2 * QW], BF16)
    tstr = sb2("tstr", [128, 2 * QW], F32)
    ev3 = sb2("ev3", [65, QW], F32)
    rrow = sb2("rrow", [1, QW], F32)
    ps_s = S2.enter_context(nc.psum_tensor("ps_s", [128, 2 * QW], F32))
    ps_acc = S2.enter_context(nc.psum_tensor("ps_acc", [65, QW], F32))
    ps_acc2 = S2.enter_context(nc.psum_tensor("ps_acc2", [65, QW], F32))
    psacs = [ps_acc, ps_acc2]
    ps_vt = S2.enter_context(nc.psum_tensor("ps_vt", [128, 1024], BF16))
    ps_r = S2.enter_context(nc.psum_tensor("ps_r", [64, QW], F32))

    p.w("gp", "dv", zdone)
    for i in range(TBLR // 128):
        store(xtable[128 * i:128 * (i + 1), :], zrows[:], eng="gp")
    gd_zero = p.cnt["gd"]

    exp_hist = {}
    texp_hist = {}
    pv_hist = {}
    fin = None
    tile_ctr = [0]
    rc_hist = {}
    pend_fin = [None]
    vtr_done = None
    for ph in range(8):
        p.w("sy", "gp", cc_kv)
        if vtr_done is not None:
            p.w("sy", "pe", vtr_done)
        for r in range(4):
            load(vfm[:, r * T:(r + 1) * T],
                 kvag_out[r * 2 * D + D + 128 * ph:
                          r * 2 * D + D + 128 * (ph + 1), :])
        ld_vfm = p.cnt["ld"]
        p.w("sy", "pe")
        p.w("sy", "ac")
        for hh in (0, 1):
            for r in range(4):
                load(kaug[hh][0:64, r * T:(r + 1) * T],
                     kvag_out[r * 2 * D + 128 * ph + 64 * hh:
                              r * 2 * D + 128 * ph + 64 * hh + 64, :])
            load(kaug[hh][64:69, :], kinds_d[:, :])
            load(kown[hh][:, :],
                 kvag_in[128 * ph + 64 * hh: 128 * ph + 64 * hh + 64, :])
        ld_kv = p.cnt["ld"]
        load(slabb[:, 0:257], slab_d[2 * ph])
        load(slabb[:, 257:514], slab_d[2 * ph + 1])
        ld_slab = p.cnt["ld"]

        p.w("dv", "ld", ld_slab)
        p.w("dv", "dv", b_done)
        p.w("dv", "ac")
        for hh in (0, 1):
            dv(lambda e, hh=hh, ph=ph: e.tensor_scalar(
                devt[:, hh * 257:(hh + 1) * 257],
                slabb[:, hh * 257:(hh + 1) * 257],
                b128[:, 2 * ph + hh:2 * ph + hh + 1], None, OP.subtract))
            dv(lambda e, hh=hh, ph=ph: e.tensor_scalar(
                strips[:, hh * 896:(hh + 1) * 896], stripm[:],
                float(SLOPES[2 * ph + hh]), None, OP.mult))
        tabs_done = p.cnt["dv"]
        if "devt0" in tap_t and ph == 7:
            p.w("sy", "dv", tabs_done)
            store(tap_t["devt0"].ap(), devt[:])

        dv(lambda e: e.memset(vstr[0][:], 1.0))
        dv(lambda e: e.memset(vstr[1][:], 1.0))
        dv(lambda e: e.memset(vostr[0][:], 1.0))
        dv(lambda e: e.memset(vostr[1][:], 1.0))
        msdone = p.cnt["dv"]
        p.w("pe", "ld", ld_vfm)
        p.w("pe", "dv", msdone)
        for g in range(8):
            for m in range(8):
                j = 8 * g + m
                f = (lambda e, j=j, m=m: e.transpose(
                    ps_vt[:, 128 * m:128 * (m + 1)],
                    vfm[:, 128 * j:128 * (j + 1)], identb[:]))
                pei(f) if m == 7 else pe(f)
            tp = p.cnt["pe"]
            p.w("dv", "pe", tp)
            for hh in (0, 1):
                src = _ap(ps_vt, 64 * hh, [[1024, 128], [128, 8], [1, 64]])
                dst = _ap(vstr[hh], 65 * 8 * g, [[65 * NKB, 128], [65, 8], [1, 64]])
                dv(lambda e, s=src, d=dst: e.tensor_copy(d, s))
            evv = p.cnt["dv"]
            p.w("pe", "dv", evv)
        vof = sb2("vof", [128, T], BF16) if ph == 0 else vof
        for hh in (0, 1):
            p.w("sy", "pe")
            lvo = load(vof[0:64, :],
                       kvag_in[D + 128 * ph + 64 * hh: D + 128 * ph + 64 * hh + 64, :])
            p.w("pe", "ld", lvo)
            p.w("pe", "dv")
            for m in range(16):
                f = (lambda e, m=m: e.transpose(
                    ps_vt[0:128, 64 * (m % 8):64 * (m % 8) + 64],
                    vof[0:64, 128 * m:128 * (m + 1)], identb[0:64, 0:64]))
                pei(f) if (m % 8 == 7) else pe(f)
                if m % 8 == 7:
                    tp = p.cnt["pe"]
                    p.w("dv", "pe", tp)
                    g0 = m // 8
                    src = _ap(ps_vt, 0, [[1024, 128], [64, 8], [1, 64]])
                    dst = _ap(vostr[hh], 65 * 8 * g0,
                              [[65 * 16, 128], [65, 8], [1, 64]])
                    dv(lambda e, s=src, d=dst: e.tensor_copy(d, s))
                    evv = p.cnt["dv"]
                    p.w("pe", "dv", evv)
        vdone = p.cnt["dv"]
        vtr_done = p.cnt["pe"]
        if ph == 7 and "vstr0" in tap_t:
            p.w("gp", "dv", vdone)
            store(tap_t["vstr0"].ap(), vstr[0][:], eng="gp")
            store(tap_t["vostr0"].ap(), vostr[0][:], eng="gp")

        for hh in (0, 1):
            for qt in range(4):
                buf = tile_ctr[0] % 2
                tile_ctr[0] += 1
                psa = psacs[buf]
                p.w("pe", "dv", vdone)
                p.w("pe", "dv", tabs_done)
                p.w("pe", "ld", ld_kv)
                prc = rc_hist.get(buf)
                if prc is not None:
                    p.w("pe", "dv", prc)


                nblk = NKB + 4

                def issue_smm(bk, hh=hh, qt=qt, ph=ph):
                    strad = bk >= NKB
                    dd = bk - NKB
                    bsl = bk % 2
                    spt = ps_s[:, bsl * QW:(bsl + 1) * QW]
                    prev = exp_hist.get(bsl)
                    if prev is not None:
                        p.w("pe", "ac", prev)
                    if not strad:
                        pei(lambda e, spt=spt, bk=bk:
                            e.matmul(spt,
                                     kaug[hh][:, 128 * bk:128 * (bk + 1)],
                                     qaug[:, (2 * ph + hh) * T + qt * QW:
                                          (2 * ph + hh) * T + (qt + 1) * QW],
                                     start=True, stop=True))
                    else:
                        pei(lambda e, spt=spt, dd=dd:
                            e.matmul(spt,
                                     kown[hh][:, QW * qt + 128 * dd:
                                              QW * qt + 128 * (dd + 1)],
                                     qaug[0:64, (2 * ph + hh) * T + qt * QW:
                                          (2 * ph + hh) * T + (qt + 1) * QW],
                                     start=True, stop=True))
                    return p.cnt["pe"]

                smm_q = [issue_smm(0)]
                for bk in range(nblk):
                    strad = bk >= NKB
                    dd = bk - NKB
                    bsl = bk % 2
                    spt = ps_s[:, bsl * QW:(bsl + 1) * QW]
                    col = (qt * NKB + bk) if not strad else 256
                    smmc = smm_q.pop(0)
                    if bk + 1 < nblk:
                        smm_q.append(issue_smm(bk + 1))
                    if bk == 2 and pend_fin[0] is not None:
                        pend_fin[0]()
                        pend_fin[0] = None
                    if strad:
                        p.w("dv", "pe", smmc)
                        prevexp = exp_hist.get(bsl)
                        if prevexp is not None:
                            p.w("dv", "ac", prevexp)
                        soff = 896 * hh + 384 - 128 * dd
                        tsl = tstr[:, bsl * QW:(bsl + 1) * QW]
                        tdv = dv(lambda e, spt=spt, soff=soff, tsl=tsl:
                                 e.tensor_tensor(tsl, spt,
                                                 strips[:, soff:soff + QW],
                                                 OP.add))
                        p.w("ac", "dv", tdv)
                        expsrc = tsl
                    else:
                        expsrc = spt
                    p.w("ac", "pe", smmc)
                    prevpv = pv_hist.get(bsl)
                    if prevpv is not None:
                        p.w("ac", "pe", prevpv)
                    expc = ac(lambda e, src=expsrc, bsl=bsl, hh=hh, col=col:
                              e.activation(ptb[:, bsl * QW:(bsl + 1) * QW], src,
                                           AF.Exp,
                                           bias=devt[:, hh * 257 + col:
                                                     hh * 257 + col + 1],
                                           scale=0.125))
                    exp_hist[bsl] = expc
                    if ph == 7 and hh == 0 and qt == 0:
                        if strad and dd == 0 and "tstr0" in tap_t:
                            p.w("gp", "dv", tdv)
                            store(tap_t["tstr0"].ap(), tstr[:, 0:QW], eng="gp")
                            p.w("gp", "ac", expc)
                            store(tap_t["pts0"].ap(),
                                  ptb[:, bsl * QW:(bsl + 1) * QW], eng="gp")
                            p.w("pe", "gd")
                        if (not strad) and bk == 8 and "ptm0" in tap_t:
                            p.w("gp", "ac", expc)
                            store(tap_t["ptm0"].ap(),
                                  ptb[:, bsl * QW:(bsl + 1) * QW], eng="gp")
                            p.w("pe", "gd")
                    p.w("pe", "ac", expc)
                    if not strad:
                        pei(lambda e, psa=psa, hh=hh, bk=bk, bsl=bsl: e.matmul(
                            psa[:, :], vstr[hh][:, 65 * bk:65 * bk + 65],
                            ptb[:, bsl * QW:(bsl + 1) * QW],
                            start=(bk == 0), stop=False,
                            skip_group_check=True))
                    else:
                        pei(lambda e, psa=psa, hh=hh, qt=qt, dd=dd, bsl=bsl:
                            e.matmul(
                            psa[:, :],
                            vostr[hh][:, 65 * (4 * qt + dd):
                                      65 * (4 * qt + dd) + 65],
                            ptb[:, bsl * QW:(bsl + 1) * QW],
                            start=False, stop=False, skip_group_check=True))
                    pv_hist[bsl] = p.cnt["pe"]

                pvall = p.cnt["pe"]
                p.w("dv", "pe", pvall)
                dv(lambda e, psa=psa: e.tensor_copy(ev3[:], psa[:, :]))
                dv(lambda e: e.reciprocal(rrow[:], ev3[64:65, :]))
                rc = p.cnt["dv"]
                rc_hist[buf] = rc
                if "ev30" in tap_t and ph == 7 and hh == 0 and qt == 0:
                    p.w("sy", "dv", rc)
                    store(tap_t["ev30"].ap(), ev3[:])
                dst = oall[64 * hh:64 * (hh + 1),
                           ph * T + qt * QW: ph * T + (qt + 1) * QW]

                def mk_fin(rc=rc, dst=dst):
                    def flush():
                        global fin
                        p.w("pe", "dv", rc)
                        pei(lambda e: e.matmul(ps_r[:, :], onesrow[:, 0:64],
                                               rrow[:], start=True, stop=True))
                        bcr = p.cnt["pe"]
                        p.w("dv", "pe", bcr)
                        dv(lambda e, d=dst: e.tensor_mul(d, ev3[0:64, :],
                                                         ps_r[:, :]))
                    return flush
                pend_fin[0] = mk_fin()

    if pend_fin[0] is not None:
        pend_fin[0]()
        pend_fin[0] = None
    if "oall" in tap_t:
        p.w("gp", "dv")
        store(tap_t["oall"].ap(), oall[:], eng="gp")
    S2.close()
    S1q.close()

    S3 = ExitStack()
    x1b = S3.enter_context(nc.sbuf_tensor("x1b", [128, 8 * T], BF16))

    S3z = ExitStack()
    sb3 = lambda n, s, d: S3z.enter_context(nc.sbuf_tensor(n, s, d))
    zf = sb3("zf", [128, 8 * T], F32)
    wob = sb3("wob", [128, 8 * D], BF16)
    boutc = sb3("boutc", [128, 8], F32)
    bvf = sb3("bvf", [128, 8], F32)
    bvb = sb3("bvb", [128, 8], BF16)
    pbias = sb3("pbias", [128, 8], F32)
    xfc = sb3("xfc", [128, 2 * 1024], F32)
    zb2 = sb3("zb2", [128, 2 * QW], BF16)
    zq2 = sb3("zq2", [128, 2 * QW], BF16)
    murow = sb3("murow", [1, QW], F32)
    varrow = sb3("varrow", [1, QW], F32)
    rsrow = sb3("rsrow", [1, QW], F32)
    tmpq = sb3("tmpq", [128, QW], F32)
    stg32 = sb3("stg32", [128, D], F32)
    ps_p = S3z.enter_context(nc.psum_tensor("ps_p", [128, 2 * QW], F32))
    ps_cm = S3z.enter_context(nc.psum_tensor("ps_cm", [128, QW], F32))
    ps_c = ps_cm
    ps_sq = S3z.enter_context(nc.psum_tensor("ps_sq", [1, QW], F32))
    ps_mb = S3z.enter_context(nc.psum_tensor("ps_mb", [128, QW], F32))
    ps_rb = S3z.enter_context(nc.psum_tensor("ps_rb", [128, QW], F32))
    ps_trA = S3z.enter_context(nc.psum_tensor("ps_trA", [128, QW], F32))
    ps_trB = S3z.enter_context(nc.psum_tensor("ps_trB", [128, QW], F32))
    ps_trh = [ps_trA, ps_trB]

    for _eng in ("sy", "gp"):
        p.w(_eng, "pe")
        p.w(_eng, "ac")
        p.w(_eng, "dv")
    for c in range(8):
        load(wob[:, c * D:(c + 1) * D], wout_d[128 * c:128 * (c + 1), :],
             eng="gp")
    gd_w = p.cnt["gd"]
    for j in range(8):
        load(boutc[:, j:j + 1], bout_d[128 * j:128 * (j + 1), :])
    for j in range(8):
        load(bvf[:, j:j + 1], bqkv_d[2 * D + 128 * j: 2 * D + 128 * (j + 1), :])
    ld_pb = p.cnt["ld"]
    p.w("dv", "ld", ld_pb)
    dv(lambda e: e.tensor_copy(bvb[:], bvf[:]))
    cvt = p.cnt["dv"]
    p.w("pe", "dv", cvt)
    p.w("pe", "gd", gd_w)
    dv(lambda e: e.tensor_copy(pbias[:], boutc[:]))
    pbias_done = p.cnt["dv"]

    p.w("pe", "dv")
    evh = {}
    for t in range(8):
        for n in range(4):
            bank = (t * 4 + n) % 2
            pst = ps_p[:, bank * QW:(bank + 1) * QW]
            prev = evh.get(bank)
            if prev is not None:
                p.w("pe", "ac", prev)
            for kc in range(8):
                f = (lambda e, t=t, n=n, kc=kc, pst=pst: e.matmul(
                    pst, wob[:, kc * D + 128 * t: kc * D + 128 * (t + 1)],
                    oall[:, kc * T + n * QW: kc * T + (n + 1) * QW],
                    start=(kc == 0), stop=(kc == 7)))
                pei(f) if kc == 7 else pe(f)
            mmc = p.cnt["pe"]
            p.w("ac", "pe", mmc)
            p.w("ac", "dv", pbias_done)
            ev = ac(lambda e, pst=pst, t=t, n=n: e.activation(
                zf[:, t * T + n * QW: t * T + (n + 1) * QW], pst, AF.Identity,
                bias=pbias[:, t:t + 1], scale=1.0))
            evh[bank] = ev
    proj_done = p.cnt["ac"]

    lx_hist = {}
    for ch in range(16):
        c, hh2 = ch // 2, ch % 2
        half = ch % 2
        prev_add = lx_hist.get(("add", half))
        if prev_add is not None:
            p.w("sy", "dv", prev_add)
        lx = load(xfc[:, half * 1024:(half + 1) * 1024],
                  xT_d[128 * c:128 * (c + 1), 1024 * hh2:1024 * (hh2 + 1)])
        p.w("dv", "ld", lx)
        p.w("dv", "ac", proj_done)
        a = dv(lambda e, c=c, hh2=hh2, half=half: e.tensor_add(
            zf[:, c * T + 1024 * hh2: c * T + 1024 * (hh2 + 1)],
            zf[:, c * T + 1024 * hh2: c * T + 1024 * (hh2 + 1)],
            xfc[:, half * 1024:(half + 1) * 1024]))
        lx_hist[("add", half)] = a
    zready = p.cnt["dv"]
    if "zf" in tap_t:
        p.w("sy", "dv", zready)
        store(tap_t["zf"].ap(), zf[:])

    for n in range(4):
        for t in range(8):
            half = t % 2
            p.w("dv", "dv", zready)
            cb = dv(lambda e, t=t, n=n, half=half: e.tensor_copy(
                zb2[:, half * QW:(half + 1) * QW],
                zf[:, t * T + n * QW: t * T + (n + 1) * QW]))
            p.w("pe", "dv", cb)
            f = (lambda e, t=t, half=half: e.matmul(
                ps_cm[0:1, :], onescol[:], zb2[:, half * QW:(half + 1) * QW],
                start=(t == 0), stop=(t == 7)))
            pei(f) if t == 7 else pe(f)
            smc = p.cnt["pe"]
            p.w("dv", "pe", smc)
        p.w("ac", "dv", zready)
        for t in range(8):
            half = t % 2
            sb_ = ac(lambda e, t=t, n=n, half=half: e.activation(
                zq2[:, half * QW:(half + 1) * QW],
                zf[:, t * T + n * QW: t * T + (n + 1) * QW], AF.Square))
            p.w("pe", "ac", sb_)
            f = (lambda e, t=t, half=half: e.matmul(
                ps_sq[:, :], onescol[:], zq2[:, half * QW:(half + 1) * QW],
                start=(t == 0), stop=(t == 7)))
            pei(f) if t == 7 else pe(f)
            smc = p.cnt["pe"]
            p.w("ac", "pe", smc)
        smc = p.cnt["pe"]
        p.w("dv", "pe", smc)
        dv(lambda e: e.tensor_scalar(murow[:], ps_cm[0:1, :], 1.0 / D, None,
                                     OP.mult))
        dv(lambda e: e.tensor_scalar(varrow[:], ps_sq[:, :], 1.0 / D, None,
                                     OP.mult))
        dv(lambda e: e.tensor_mul(rsrow[:], murow[:], murow[:]))
        dv(lambda e: e.tensor_sub(varrow[:], varrow[:], rsrow[:]))
        vd = p.cnt["dv"]
        if "murows" in tap_t:
            p.w("sy", "dv", vd)
            store(tap_t["murows"].ap()[n:n + 1, :], murow[:])
            store(tap_t["varrows"].ap()[n:n + 1, :], varrow[:])
            p.w("dv", "st")
        p.w("pe", "dv", vd)
        p.w("ac", "dv", vd)
        sdv = ac(lambda e: e.activation(rsrow[:], varrow[:], AF.Sqrt, bias=eps128[0:1, 0:1]))
        p.w("dv", "ac", sdv)
        dv(lambda e: e.reciprocal(rsrow[:], rsrow[:]))
        rowsdone = p.cnt["dv"]
        p.w("pe", "dv", rowsdone)
        pe(lambda e, n=n: e.matmul(ps_mb[:, :], onesrow[:], murow[:],
                                   start=True, stop=True))
        pei(lambda e, n=n: e.matmul(ps_rb[:, :], onesrow[:], rsrow[:],
                                    start=True, stop=True))
        bcc = p.cnt["pe"]
        p.w("dv", "pe", bcc)
        for t in range(8):
            dv(lambda e, t=t, n=n: e.tensor_sub(
                tmpq[:], zf[:, t * T + n * QW: t * T + (n + 1) * QW],
                ps_mb[:, :]))
            dv(lambda e, t=t, n=n: e.tensor_mul(
                zf[:, t * T + n * QW: t * T + (n + 1) * QW], tmpq[:],
                ps_rb[:, :]))
        lnc_ = p.cnt["dv"]
        p.w("pe", "dv", lnc_)
    x1done = p.cnt["dv"]
    if "x1" in tap_t:
        p.w("sy", "dv", x1done)
        store(tap_t["x1"].ap(), zf[:])

    for c in range(8):
        dv(lambda e, c=c: e.tensor_copy(x1b[:, c * T:(c + 1) * T],
                                        zf[:, c * T:(c + 1) * T]))
    x1b_done = p.cnt["dv"]

    if "zft" in tap_t:
        p.w("sy", "dv", x1done)
        store(tap_t["zft"].ap(), zf[:, 0:128])
    if "idf" in tap_t:
        store(tap_t["idf"].ap(), identf[:])
    tspare = None
    if "pstr0" in tap_t:
        tspare = S3z.enter_context(nc.sbuf_tensor("tspare", [128, 512], F32))
    p.w("pe", "dv", x1done)
    st_x1 = None
    evc = None
    for tt in range(16):
        if st_x1 is not None:
            p.w("pe", "st", st_x1)
        for half in range(2):
            for q in range(4):
                c = half * 4 + q
                f = (lambda e, tt=tt, c=c, half=half, q=q: e.transpose(
                    ps_trh[half][:, 128 * q:128 * (q + 1)],
                    zf[:, c * T + 128 * tt: c * T + 128 * (tt + 1)], identf[:]))
                pei(f) if q == 3 else pe(f)
            trc = p.cnt["pe"]
            p.w("dv", "pe", trc)
            if tt == 0 and half == 0 and tspare is not None:
                p.w("ac", "pe", trc)
                acc_ = ac(lambda e: e.activation(tspare[:], ps_trA[:, :], AF.Copy))
                p.w("sy", "ac", acc_)
                store(tap_t["pstr0"].ap(), tspare[:])
                p.w("pe", "ac", acc_)
            dv(lambda e, half=half: e.tensor_copy(
                stg32[:, half * QW:(half + 1) * QW], ps_trh[half][:, :]))
            evc = p.cnt["dv"]
            p.w("pe", "dv", evc)
        p.w("sy", "dv", evc)
        if tt == 0 and "st32" in tap_t:
            store(tap_t["st32"].ap(), stg32[:])
        st_x1 = store(x1spill[128 * tt:128 * (tt + 1), :], stg32[:])
        st_x1 = p.cnt["st"]
    S3z.close()

    S4 = ExitStack()
    sb4 = lambda n, s, d: S4.enter_context(nc.sbuf_tensor(n, s, d))
    rwT = sb4("rwT", [128, 64], BF16)
    rbt = sb4("rbt", [128, 8], F32)
    ltri = sb4("ltri", [128, 128], F32)
    ltok = sb4("ltok", [128, 16 * 8], F32)
    gstrip = sb4("gstrip", [128, 16 * 8], F32)
    mstrip = sb4("mstrip", [128, 16 * 8], F32)
    grtok = sb4("grtok", [128, 16 * 8], F32)
    totrow = sb4("totrow", [1, 8], F32)
    prevrow = sb4("prevrow", [1, 8], F32)
    crow = sb4("crow", [1, 64], F32)
    cmask = sb4("cmask", [1, 64], F32)
    offrow = sb4("offrow", [1, 8], F32)
    agb = sb4("agb", [128, 8], F32)
    cin = sb4("cin", [8, 8], F32)
    ccol = sb4("ccol", [8, 1], F32)
    t41 = sb4("t41", [128, 8], F32)
    t42 = sb4("t42", [128, 8], F32)
    t43 = sb4("t43", [128, 8], F32)
    m1c = sb4("m1c", [128, 1], F32)
    m2c = sb4("m2c", [128, 1], F32)
    oc1 = sb4("oc1", [128, 1], F32)
    oc2 = sb4("oc2", [128, 1], F32)
    g1c = sb4("g1c", [128, 1], F32)
    g2c = sb4("g2c", [128, 1], F32)
    stg1 = sb4("stg1", [128, D], BF16)
    stg2 = sb4("stg2", [128, D], BF16)
    x4t = sb4("x4t", [128, D], F32)
    oint1 = sb4("oint1", [128, 1], I32)
    oint2 = sb4("oint2", [128, 1], I32)
    ps_l8 = S4.enter_context(nc.psum_tensor("ps_l8", [128, 8], F32))
    ps_rk = S4.enter_context(nc.psum_tensor("ps_rk", [128, 8], F32))
    ps_tot = S4.enter_context(nc.psum_tensor("ps_tot", [1, 8], F32))
    ps_ab = S4.enter_context(nc.psum_tensor("ps_ab", [128, 8], F32))

    for _eng in ("sy", "gp"):
        p.w(_eng, "pe")
        p.w(_eng, "ac")
        p.w(_eng, "dv")
        p.w(_eng, "st")
    if "x1sp4" in tap_t:
        store(tap_t["x1sp4"].ap(), x1spill[0:128, :], eng="gp")
    ld_rwt = load(rwT[:], rwt_d[:, :], eng="gp")
    ld_rb = load(rbt[:], rbt_d[:, :])
    ld_lt = load(ltri[:], ltri_d[:, :])

    p.w("pe", "gd", ld_rwt)
    p.w("pe", "dv", x1b_done)
    evl = None
    for tt in range(16):
        if evl is not None:
            p.w("pe", "dv", evl)
        for kc in range(8):
            f = (lambda e, tt=tt, kc=kc: e.matmul(
                ps_l8[:, :], x1b[:, kc * T + 128 * tt: kc * T + 128 * (tt + 1)],
                rwT[:, 8 * kc:8 * (kc + 1)], start=(kc == 0), stop=(kc == 7)))
            pei(f) if kc == 7 else pe(f)
        lmc = p.cnt["pe"]
        p.w("dv", "pe", lmc)
        p.w("dv", "ld", ld_rb)
        evl = dv(lambda e, tt=tt: e.tensor_add(
            ltok[:, 8 * tt:8 * (tt + 1)], ps_l8[:, :], rbt[:]))
    ltok_done = p.cnt["dv"]
    if "ltok" in tap_t:
        p.w("sy", "dv", ltok_done)
        store(tap_t["ltok"].ap(), ltok[:])

    for tt in range(16):
        lt = ltok[:, 8 * tt:8 * (tt + 1)]
        dv(lambda e, lt=lt: e.tensor_reduce(m1c[:], lt, AX, OP.max))
        dv(lambda e: e.tensor_scalar_mul(m2c[:], m1c[:], -1.0))
        md = p.cnt["dv"]
        p.w("ac", "dv", md)
        ec = ac(lambda e, lt=lt: e.activation(t41[:], lt, AF.Exp,
                                              bias=m2c[:, 0:1], scale=1.0))
        p.w("dv", "ac", ec)
        dv(lambda e: e.tensor_reduce(g1c[:], t41[:], AX, OP.add))
        dv(lambda e: e.reciprocal(g1c[:], g1c[:]))
        dv(lambda e: e.tensor_scalar(t41[:], t41[:], g1c[:, 0:1], None, OP.mult))
        dv(lambda e, lt=lt: e.tensor_scalar(t42[:], lt, m1c[:, 0:1], None,
                                            OP.is_ge))
        dv(lambda e, lt=lt: e.scalar_tensor_tensor(t43[:], t42[:], -1.0e30, lt,
                                                   OP.mult, OP.add))
        dv(lambda e: e.tensor_reduce(m2c[:], t43[:], AX, OP.max))
        dv(lambda e, lt=lt: e.tensor_scalar(t42[:], lt, m2c[:, 0:1], None,
                                            OP.is_ge))
        dv(lambda e, tt=tt: e.tensor_mul(gstrip[:, 8 * tt:8 * (tt + 1)],
                                         t41[:], t42[:]))
        dv(lambda e, tt=tt: e.tensor_copy(mstrip[:, 8 * tt:8 * (tt + 1)],
                                          t42[:]))
        dv(lambda e, tt=tt: e.tensor_scalar(npsc[:, tt:tt + 1],
                                            gstrip[:, 8 * tt + 7:8 * tt + 8],
                                            1.0, None, OP.add))
    top2_done = p.cnt["dv"]
    if "gates" in tap_t:
        p.w("sy", "dv", top2_done)
        store(tap_t["gates"].ap(), gstrip[:])

    dv(lambda e: e.memset(prevrow[:], 0.0))
    pr_done = p.cnt["dv"]
    p.w("pe", "dv", top2_done)
    p.w("pe", "ld", ld_lt)
    for tt in range(16):
        p.w("pe", "dv", pr_done)
        pe(lambda e, tt=tt: e.matmul(
            ps_rk[:, :], ltri[:], mstrip[:, 8 * tt:8 * (tt + 1)],
            start=True, stop=False, skip_group_check=True))
        pei(lambda e, tt=tt: e.matmul(
            ps_rk[:, :], onesrow[:], prevrow[:],
            start=False, stop=True, skip_group_check=True))
        rkc = p.cnt["pe"]
        pei(lambda e, tt=tt: e.matmul(
            ps_tot[:, :], onescol128f[:], mstrip[:, 8 * tt:8 * (tt + 1)],
            start=(tt == 0), stop=(tt == 15)))
        totc = p.cnt["pe"]
        p.w("dv", "pe", rkc)
        dv(lambda e, tt=tt: e.tensor_copy(grtok[:, 8 * tt:8 * (tt + 1)],
                                          ps_rk[:, :]))
        p.w("dv", "pe", totc)
        dv(lambda e: e.tensor_copy(totrow[:], ps_tot[:, :]))
        pr_done = dv(lambda e: e.tensor_add(prevrow[:], totrow[:], totrow[:]))
        pr_done = dv(lambda e: e.tensor_copy(prevrow[:], totrow[:]))
    lrank_done = p.cnt["dv"]

    p.w("sy", "dv", lrank_done)
    st_ct = store(scr_d.ap()[40:41, 0:8], totrow[:])
    p.w("sy", "st", st_ct)
    ld_cc = load(ccol[:], _ap(scr_d.ap(), 40 * 8, [[1, 8], [8, 1]]), slow=True)
    dv(lambda e: e.memset(cin[:], 0.0))
    p.w("dv", "ld", ld_cc)
    cv = dv(lambda e: e.tensor_copy(cin[:, 0:1], ccol[:]))
    p.w("gp", "dv", cv)
    store(cntag_in.ap(), cin[:], eng="gp")
    p.w("gp", "gd")
    cc_cnt = gpc(lambda e: e.collective_compute(
        "AllGather", OP.bypass, replica_groups=[[0, 1, 2, 3, 4, 5, 6, 7]],
        ins=[cntag_in.ap()], outs=[cntag_out.ap()]))
    p.w("sy", "gp", cc_cnt)
    ld_cr = load(crow[:], _ap(cntag_out.ap(), 0, [[512, 1], [8, 64]]), slow=True)
    p.w("dv", "ld", ld_cr)
    dv(lambda e: e.tensor_mul(cmask[:], crow[:], agm[:]))
    for e_ in range(7):
        src_ = _ap(cmask, e_, [[64, 1], [8, 8]])
        dv(lambda e, s=src_, e_=e_: e.tensor_reduce(offrow[:, e_:e_ + 1], s,
                                                    AX, OP.add))
    dv(lambda e: e.memset(offrow[:, 7:8], 0.0))
    ofr = p.cnt["dv"]
    p.w("pe", "dv", ofr)
    pei(lambda e: e.matmul(ps_ab[:, :], onesrow[:], offrow[:],
                           start=True, stop=True))
    abc = p.cnt["pe"]
    p.w("dv", "pe", abc)
    dv(lambda e: e.tensor_copy(agb[:], ps_ab[:, :]))
    for tt in range(16):
        dv(lambda e, tt=tt: e.tensor_add(grtok[:, 8 * tt:8 * (tt + 1)],
                                         grtok[:, 8 * tt:8 * (tt + 1)], agb[:]))
    grt_done = p.cnt["dv"]
    if "grtok" in tap_t:
        p.w("sy", "dv", grt_done)
        store(tap_t["grtok"].ap(), grtok[:])

    sc_hist = None
    for tt in range(16):
        gr = grtok[:, 8 * tt:8 * tt + 7]
        mk = mstrip[:, 8 * tt:8 * tt + 7]
        gt = gstrip[:, 8 * tt:8 * tt + 7]
        dv(lambda e, gr=gr: e.tensor_scalar(t41[:, 0:7], gr, float(CAP), None,
                                            OP.is_lt))
        dv(lambda e, mk=mk: e.tensor_mul(t41[:, 0:7], t41[:, 0:7], mk))
        dv(lambda e, gr=gr: e.tensor_add(t42[:, 0:7], gr, eoff[:, 0:7]))
        dv(lambda e: e.tensor_scalar(t42[:, 0:7], t42[:, 0:7], float(DUMP_X),
                                     None, OP.subtract))
        dv(lambda e: e.tensor_mul(t42[:, 0:7], t42[:, 0:7], t41[:, 0:7]))
        dv(lambda e: e.tensor_scalar(t42[:, 0:7], t42[:, 0:7], float(DUMP_X),
                                     None, OP.add))
        dv(lambda e: e.tensor_reduce(oc1[:], t42[:, 0:7], AX, OP.min))
        dv(lambda e: e.tensor_scalar(t43[:, 0:7], t42[:, 0:7], oc1[:, 0:1],
                                     None, OP.is_equal))
        dv(lambda e, gt=gt: e.tensor_mul(t41[:, 0:7], t43[:, 0:7], gt))
        dv(lambda e: e.tensor_reduce(g1c[:], t41[:, 0:7], AX, OP.add))
        dv(lambda e: e.scalar_tensor_tensor(t43[:, 0:7], t43[:, 0:7], 30000.0,
                                            t42[:, 0:7], OP.mult, OP.add))
        dv(lambda e: e.tensor_reduce(oc2[:], t43[:, 0:7], AX, OP.min))
        dv(lambda e: e.tensor_scalar_min(oc2[:], oc2[:], float(DUMP_X)))
        dv(lambda e: e.tensor_scalar(t41[:, 0:7], t42[:, 0:7], oc2[:, 0:1],
                                     None, OP.is_equal))
        dv(lambda e, gt=gt: e.tensor_mul(t41[:, 0:7], t41[:, 0:7], gt))
        dv(lambda e: e.tensor_reduce(g2c[:], t41[:, 0:7], AX, OP.add))
        dv(lambda e: e.tensor_copy(oint1[:], oc1[:]))
        dv(lambda e: e.tensor_copy(oint2[:], oc2[:]))
        dv(lambda e: e.tensor_scalar(t41[:, 0:1], oc1[:], float(DUMP_X), None,
                                     OP.is_ge))
        dv(lambda e: e.scalar_tensor_tensor(
            t42[:, 0:1], t41[:, 0:1], float(DUMP_G - DUMP_X), oc1[:],
            OP.mult, OP.add))
        dv(lambda e: e.tensor_scalar(t41[:, 1:2], oc2[:], float(DUMP_X), None,
                                     OP.is_ge))
        dv(lambda e: e.scalar_tensor_tensor(
            t42[:, 1:2], t41[:, 1:2], float(DUMP_G - DUMP_X), oc2[:],
            OP.mult, OP.add))
        dv(lambda e, tt=tt: e.tensor_copy(offi[:, 2 * tt:2 * tt + 2],
                                          t42[:, 0:2]))
        if sc_hist is not None:
            p.w("dv", "gd", sc_hist)
        p.w("sy", "dv")
        lx4 = load(x4t[:], x1spill[128 * tt:128 * (tt + 1), :])
        p.w("dv", "ld", lx4)
        if tt == 0 and "x4t" in tap_t:
            p.w("sy", "ld", lx4)
            store(tap_t["x4t"].ap(), x4t[:])
            p.w("dv", "st")
        dv(lambda e, tt=tt: e.tensor_scalar(stg1[:], x4t[:],
                                            g1c[:, 0:1], None, OP.mult))
        dv(lambda e, tt=tt: e.tensor_scalar(stg2[:], x4t[:],
                                            g2c[:, 0:1], None, OP.mult))
        stgd = p.cnt["dv"]
        if tt == 0 and "stg1" in tap_t:
            p.w("gp", "dv", stgd)
            store(tap_t["stg1"].ap(), stg1[:], eng="gp")
            gtmp = dv(lambda e: e.tensor_copy(t43[:, 0:1], g1c[:]))
            gtmp = dv(lambda e: e.tensor_copy(t43[:, 1:2], g2c[:]))
            p.w("gp", "dv", gtmp)
            store(tap_t["g1"].ap(), t43[:, 0:2], eng="gp")
            p.w("dv", "gd")
        p.w("gp", "dv", stgd)
        p.w("gp", "gd", gd_zero)
        p.op("gp", lambda e: e.indirect_dma_start(
            out=xtable.ap(), out_offset=IndirectOffsetOnAxis(oint1[:], 0),
            in_=stg1[:], in_offset=None), inc="gd", n=16)
        sc_hist = p.op("gp", lambda e: e.indirect_dma_start(
            out=xtable.ap(), out_offset=IndirectOffsetOnAxis(oint2[:], 0),
            in_=stg2[:], in_offset=None), inc="gd", n=16)
    scat_done = p.cnt["gd"]
    offi_done = p.cnt["dv"]
    if "offs" in tap_t:
        offs_f = S4.enter_context(nc.sbuf_tensor("offs_f", [128, 32], F32))
        p.w("dv", "dv", offi_done)
        cof = dv(lambda e: e.tensor_copy(offs_f[:], offi[:]))
        p.w("sy", "dv", cof)
        store(tap_t["offs"].ap(), offs_f[:])

    p.w("gp", "gd", scat_done)
    if "xtb" in tap_t:
        store(tap_t["xtb"].ap(), xtable[2048:2304, :], eng="gp")
    cc_rs = gpc(lambda e: e.collective_compute(
        "ReduceScatter", OP.add, replica_groups=[[0, 1, 2, 3, 4, 5, 6, 7]],
        ins=[xtable.ap()], outs=[rsout_d.ap()]))
    S4.close()
    S3.close()
    S_o.close()

    S5 = ExitStack()
    sb5 = lambda n, s, d: S5.enter_context(nc.sbuf_tensor(n, s, d))
    wibh0 = sb5("wibh0", [128, 8 * 2048], BF16)
    wibh1 = sb5("wibh1", [128, 8 * 2048], BF16)
    wibhs = [wibh0, wibh1]
    hstr = sb5("hstr", [128, 32 * 1024], BF16)
    xgT = sb5("xgT", [128, 8 * 1024], BF16)
    rstl = sb5("rstl", [128, 2 * 1024], BF16)
    wos = sb5("wos", [128, 32 * 256], BF16)
    ofm = sb5("ofm", [128, 8 * 1024], BF16)
    stg5 = sb5("stg5", [128, D], BF16)
    ps_i = S5.enter_context(nc.psum_tensor("ps_i", [128, 2 * QW], F32))
    ps_o = S5.enter_context(nc.psum_tensor("ps_o", [128, 4 * QW], F32))
    ps_xb = S5.enter_context(nc.psum_tensor("ps_xb", [128, 1024], BF16))

    for halfw in range(2):
        for kc in range(8):
            load(wibhs[halfw][:, kc * 2048:(kc + 1) * 2048],
                 wi_d[128 * kc:128 * (kc + 1),
                      2048 * halfw:2048 * (halfw + 1)], eng="gp")
    gd_wi_all = p.cnt["gd"]
    wo_pre = load(wos[:].rearrange("a (b c) -> a b c", b=32),
                  _ap(wo_d.ap(), 0, [[1024, 128], [131072, 32], [1, 256]]),
                  eng="gp")
    if "x1sp" in tap_t:
        p.w("gp", "st")
        store(tap_t["x1sp"].ap(), x1spill[0:256, :], eng="gp")
    if "rsx" in tap_t:
        p.w("gp", "gp")
        store(tap_t["rsx"].ap(), rsout_d[0:256, :], eng="gp")
    if "rsx2" in tap_t:
        p.w("gp", "gp")
        store(tap_t["rsx2"].ap(), rsout_d[1024 + 768:1024 + 1024, :], eng="gp")
    for quad in range(2):
        p.w("sy", "gp", cc_rs)
        for i in range(8):
            half = i % 2
            p.w("sy", "pe")
            ld5 = load(rstl[:, half * 1024:(half + 1) * 1024],
                       rsout_d[1024 * quad + 128 * i:
                               1024 * quad + 128 * (i + 1), :])
            p.w("pe", "ld", ld5)
            p.w("pe", "dv")
            for c in range(8):
                f = (lambda e, c=c, half=half: e.transpose(
                    ps_xb[:, 128 * c:128 * (c + 1)],
                    rstl[:, half * 1024 + 128 * c: half * 1024 + 128 * (c + 1)],
                    identb[:]))
                pei(f) if c == 7 else pe(f)
            trc = p.cnt["pe"]
            p.w("dv", "pe", trc)
            src = _ap(ps_xb, 0, [[1024, 128], [128, 8], [1, 128]])
            dst = _ap(xgT, 128 * i, [[8 * 1024, 128], [1024, 8], [1, 128]])
            dv(lambda e, s=src, d=dst: e.tensor_copy(d, s))
        xgt_done = p.cnt["dv"]

        if quad == 0 and "wib" in tap_t:
            p.w("gp", "gd", gd_wi_all)
            store(tap_t["wib"].ap(), wibh0[:, 0:2048], eng="gp")
        if quad == 0 and "wib2" in tap_t:
            p.w("gp", "gd", gd_wi_all)
            store(tap_t["wib2"].ap(), wibh1[:, 7 * 2048:8 * 2048], eng="gp")
        if quad == 0 and "xgt" in tap_t:
            p.w("gp", "dv", xgt_done)
            store(tap_t["xgt"].ap(), xgT[:, 0:1024], eng="gp")
        rel_hist = {}
        for halfw in range(2):
            wibh = wibhs[halfw]
            p.w("pe", "gd", gd_wi_all)
            p.w("pe", "dv", xgt_done)
            for oc_ in range(16):
                oc = 16 * halfw + oc_
                for st in range(2):
                    bank = (2 * oc_ + st) % 2
                    pst = ps_i[:, bank * QW:(bank + 1) * QW]
                    prev = rel_hist.get(bank)
                    if prev is not None:
                        p.w("pe", "ac", prev)
                    for kc in range(8):
                        f = (lambda e, oc_=oc_, st=st, kc=kc, pst=pst,
                             wibh=wibh: e.matmul(
                            pst,
                            wibh[:, kc * 2048 + 128 * oc_: kc * 2048 + 128 * (oc_ + 1)],
                            xgT[:, kc * 1024 + st * QW: kc * 1024 + (st + 1) * QW],
                            start=(kc == 0), stop=(kc == 7)))
                        pei(f) if kc == 7 else pe(f)
                    mm5 = p.cnt["pe"]
                    p.w("ac", "pe", mm5)
                    rl = ac(lambda e, oc=oc, st=st, pst=pst: e.activation(
                        hstr[:, oc * 1024 + st * QW: oc * 1024 + (st + 1) * QW],
                        pst, AF.Relu))
                    rel_hist[bank] = rl
        relu_done = p.cnt["ac"]
        if "hq" in tap_t and quad == 0:
            p.w("gp", "ac", relu_done)
            store(tap_t["hq"].ap(), hstr[:, 0:8 * 1024], eng="gp")

        p.w("pe", "ac", relu_done)
        for ofch in range(4):
            if quad == 0 and ofch == 0:
                ldw = wo_pre
            else:
                p.w("gp", "pe")
                src_ap = _ap(wo_d.ap(), 256 * ofch,
                             [[1024, 128], [131072, 32], [1, 256]])
                ldw = load(wos[:].rearrange("a (b c) -> a b c", b=32), src_ap,
                           eng="gp")
            p.w("pe", "gd", ldw)
            for ofq in range(2):
                for st in range(2):
                    boff = (2 * ofq + st) * QW
                    for ic in range(32):
                        f = (lambda e, ic=ic, ofq=ofq, st=st, boff=boff:
                             e.matmul(
                                 _ap(ps_o, boff, [[4 * QW, 128], [1, QW]]),
                                 wos[:, ic * 256 + 128 * ofq:
                                     ic * 256 + 128 * (ofq + 1)],
                                 hstr[:, ic * 1024 + st * QW:
                                      ic * 1024 + (st + 1) * QW],
                                 start=(ic == 0), stop=(ic == 31)))
                        pei(f) if ic == 31 else pe(f)
            omm = p.cnt["pe"]
            p.w("dv", "pe", omm)
            for ofq in range(2):
                for st in range(2):
                    boff = (2 * ofq + st) * QW
                    ofc = 2 * ofch + ofq
                    dv(lambda e, boff=boff, ofc=ofc, st=st: e.tensor_copy(
                        ofm[:, ofc * 1024 + st * QW: ofc * 1024 + (st + 1) * QW],
                        _ap(ps_o, boff, [[4 * QW, 128], [1, QW]])))
            evo = p.cnt["dv"]
            p.w("pe", "dv", evo)
        ofm_done = p.cnt["dv"]

        st5 = None
        for i4 in range(8):
            if st5 is not None:
                p.w("dv", "st", st5)
            p.w("pe", "dv", ofm_done)
            p.w("pe", "dv")
            for c in range(8):
                f = (lambda e, i4=i4, c=c: e.transpose(
                    ps_xb[:, 128 * c:128 * (c + 1)],
                    ofm[:, c * 1024 + 128 * i4: c * 1024 + 128 * (i4 + 1)],
                    identb[:]))
                pei(f) if c == 7 else pe(f)
            trc5 = p.cnt["pe"]
            p.w("dv", "pe", trc5)
            dv(lambda e: e.tensor_copy(stg5[:], ps_xb[:, :]))
            evs = p.cnt["dv"]
            p.w("sy", "dv", evs)
            st5 = store(outag_in[1024 * quad + 128 * i4:
                                 1024 * quad + 128 * (i4 + 1), :], stg5[:])
    exp_done = p.cnt["st"]

    p.w("gp", "st", exp_done)
    if "eout" in tap_t:
        store(tap_t["eout"].ap(), outag_in[0:256, :], eng="gp")
        p.w("gp", "gd")
    cc_out = gpc(lambda e: e.collective_compute(
        "AllGather", OP.bypass, replica_groups=[[0, 1, 2, 3, 4, 5, 6, 7]],
        ins=[outag_in.ap()], outs=[outag_out.ap()[0:TBLR, :]]))
    S5.close()

    S6 = ExitStack()
    sb6 = lambda n, s, d: S6.enter_context(nc.sbuf_tensor(n, s, d))
    g1r = sb6("g1r", [128, D], BF16)
    g2r = sb6("g2r", [128, D], BF16)
    x1t = sb6("x1t", [128, D], F32)
    z2 = sb6("z2", [128, D], F32)
    sqs = sb6("sqs", [128, D], F32)
    s1c = sb6("s1c", [128, 1], F32)
    s2c = sb6("s2c", [128, 1], F32)
    muc = sb6("muc", [128, 1], F32)
    vrc = sb6("vrc", [128, 1], F32)
    rsc = sb6("rsc", [128, 1], F32)
    ytile = sb6("ytile", [128, D], F32)

    for _eng in ("sy", "gp"):
        p.w(_eng, "pe")
        p.w(_eng, "ac")
        p.w(_eng, "dv")
    st_y = None
    prev_z = None
    for tt in range(16):
        p.w("gp", "gp")
        p.w("gp", "dv", offi_done)
        if prev_z is not None:
            p.w("gp", "dv", prev_z)
        gl1 = p.op("gp", lambda e, tt=tt: e.indirect_dma_start(
            out=g1r[:], out_offset=None, in_=outag_out.ap(),
            in_offset=IndirectOffsetOnAxis(offi[:, 2 * tt:2 * tt + 1], 0)),
            inc="gd", n=16)
        gl2 = p.op("gp", lambda e, tt=tt: e.indirect_dma_start(
            out=g2r[:], out_offset=None, in_=outag_out.ap(),
            in_offset=IndirectOffsetOnAxis(offi[:, 2 * tt + 1:2 * tt + 2], 0)),
            inc="gd", n=16)
        if st_y is not None:
            p.w("sy", "dv")
        lx = load(x1t[:], x1spill[128 * tt:128 * (tt + 1), :])
        p.w("dv", "gd", gl2)
        p.w("dv", "ld", lx)
        if st_y is not None:
            p.w("dv", "st", st_y)
        dv(lambda e, tt=tt: e.tensor_scalar(z2[:], x1t[:], npsc[:, tt:tt + 1],
                                            None, OP.mult))
        dv(lambda e: e.tensor_add(z2[:], z2[:], g1r[:]))
        dv(lambda e: e.tensor_add(z2[:], z2[:], g2r[:]))
        prev_z = p.cnt["dv"]
        dv(lambda e: e.tensor_reduce(s1c[:], z2[:], AX, OP.add))
        zc = p.cnt["dv"]
        p.w("ac", "dv", zc)
        sq6 = ac(lambda e: e.activation(sqs[:], z2[:], AF.Square,
                                        accum_out=s2c[:, 0:1]))
        p.w("dv", "ac", sq6)
        dv(lambda e: e.tensor_scalar_mul(muc[:], s1c[:], 1.0 / D))
        dv(lambda e: e.tensor_mul(vrc[:], muc[:], muc[:]))
        dv(lambda e: e.scalar_tensor_tensor(vrc[:], s2c[:], 1.0 / D, vrc[:],
                                            OP.mult, OP.subtract))
        vc6 = p.cnt["dv"]
        p.w("ac", "dv", vc6)
        sd6 = ac(lambda e: e.activation(rsc[:], vrc[:], AF.Sqrt, bias=eps128[:, 0:1]))
        p.w("dv", "ac", sd6)
        dv(lambda e: e.reciprocal(rsc[:], rsc[:]))
        dv(lambda e: e.tensor_scalar(ytile[:], z2[:], muc[:, 0:1], rsc[:, 0:1],
                                     OP.subtract, OP.mult))
        yc = p.cnt["dv"]
        p.w("sy", "dv", yc)
        st_y = store(y_d[128 * tt:128 * (tt + 1), :], ytile[:])

    p.w("sy", "st", p.cnt["st"])
    p.w("sy", "gd", p.cnt["gd"])
    p.w("sy", "ld", p.cnt["ld"])
    S6.close()
    S0.close()

    p.emit()
    p.close()
    return nc, tap_t



def _prep_inputs(inputs):
    x = np.asarray(inputs["L0_attn_x"], np.float32).reshape(NTOK, D)
    wqkv = np.ascontiguousarray(np.asarray(inputs["L0_attn_Wqkv"], np.float32))
    bqkv = np.ascontiguousarray(
        np.asarray(inputs["L0_attn_bqkv"], np.float32).reshape(3 * D, 1))
    wout = np.ascontiguousarray(np.asarray(inputs["L0_attn_Wout"], np.float32))
    bout = np.ascontiguousarray(
        np.asarray(inputs["L0_attn_bout"], np.float32).reshape(D, 1))
    rw = np.asarray(inputs["L0_ffn_router_w"], np.float32)
    rb = np.asarray(inputs["L0_ffn_router_b"], np.float32)
    rwT = np.zeros((128, 64), np.float32)
    for kc in range(8):
        rwT[:, 8 * kc:8 * (kc + 1)] = rw[:, 128 * kc:128 * (kc + 1)].T
    rbtile = np.tile(rb.reshape(1, NE), (128, 1)).astype(np.float32)
    jt = np.arange(128)
    lt_tri = (jt[:, None] < jt[None, :]).astype(np.float32)
    wi = np.asarray(inputs["L0_ffn_experts_inter"], np.float32)
    wo = np.asarray(inputs["L0_ffn_experts_out"], np.float32)
    strip, ident, eoff, onesrow, onescol, sel2 = _host_shared()
    in_maps = []
    for c in range(NCORES):
        beta, cints, agmask, kinds, augq = _host_tables(c)
        xs = np.ascontiguousarray(x[T * c:T * (c + 1)].T)
        ex = c if c < NRE else 0
        in_maps.append({
            "xT": xs, "wqkv": wqkv, "bqkv": bqkv,
            "wout": wout, "bout": bout, "rwTh": rwT, "rbtile": rbtile, "lt_tri": lt_tri,
            "my_wi": np.ascontiguousarray(wi[ex]),
            "my_wo": np.ascontiguousarray(wo[ex]),
            "betaslab": beta, "coreints": cints,
            "agmask": agmask, "stripm": strip, "kinds": kinds, "augq": augq,
            "identbf": ident, "eofftile": eoff, "onesrow": onesrow,
            "onescol": onescol, "sel2": sel2,
        })
    return in_maps


def run(taps=(), inputs=None, **kw):
    nc, tap_t = build_program(taps)
    in_maps = _prep_inputs(inputs)
    return run_bass_kernel_spmd(nc, in_maps, core_ids=list(range(NCORES)), **kw)


def kernel(**inputs):
    res = run(taps=(), inputs=inputs)
    y = np.concatenate([r["y"] for r in res.results], axis=0)
    return y.reshape(2, SEQ, D)
